# version 17
# baseline (speedup 1.0000x reference)
import numpy as np
from contextlib import ExitStack
import concourse.bass as bass
import concourse.mybir as mybir
from concourse.bass_utils import run_bass_kernel_spmd

F32 = mybir.dt.float32
BF16 = mybir.dt.bfloat16
AF = mybir.ActivationFunctionType
ALU = mybir.AluOpType

D, T, DI, DS, DTR, NE = 1024, 2048, 2048, 16, 64, 8
NKC = D // 128
NCI = DI // 128
NTT = T // 512
CAP = 2560.0
EPS = 1e-5

ENGS = ("sync", "tensor", "vector", "scalar", "gpsimd")
SEMS = ("SX", "SG", "SM", "SA", "SV", "SP", "SC")


class Emitter:
    def __init__(self):
        self.sched = []
        self.cnt = {s: 0 for s in SEMS}

    def c(self, sem):
        return self.cnt[sem]

    def add(self, eng, fn, waits=(), inc=None, amt=1):
        waits = [(s, v) for (s, v) in waits if v > 0]
        for s, v in waits:
            assert v <= self.cnt[s], f"wait on future: {s} {v} > {self.cnt[s]}"
        if inc is not None:
            self.cnt[inc] += amt
        self.sched.append((eng, tuple(waits), fn, inc, amt))
        return self.cnt[inc] if inc is not None else None

    def emit_stream(self, eng_name, eng, sems):
        lastw = {}
        for item_eng, waits, fn, inc, amt in self.sched:
            if item_eng != eng_name:
                continue
            for s, v in waits:
                if lastw.get(s, 0) < v:
                    eng.wait_ge(sems[s], v)
                    lastw[s] = v
            inst = fn(eng) if fn is not None else None
            if inc is not None:
                inst.then_inc(sems[inc], amt)


def build(stop=99, dev_out=()):
    nc = bass.Bass("TRN2", target_bir_lowering=False, debug=False)

    def dt_(name, shape, dtype, kind="ExternalInput"):
        return nc.dram_tensor(name, shape, dtype, kind=kind)

    def scratch(name, shape, dtype):
        kind = "ExternalOutput" if name in dev_out else "Internal"
        return nc.dram_tensor(name, shape, dtype, kind=kind)

    x_h = dt_("x", [D, T], F32)
    win_h = dt_("win", [32, NKC, 128, 128], F32)
    wxp_h = dt_("wxp", [NCI, 128, 96], F32)
    wdt_h = dt_("wdt", [NCI, 64, 128], F32)
    wout_h = dt_("wout", [NCI, 8, 128, 128], F32)
    wglu_h = dt_("wglu", [16, NKC, 128, 128], F32)
    w1_h = dt_("w1", [NE, 4, 8, NKC, 128, 128], F32)
    w2_h = dt_("w2", [NE, 4, 8, 8, 128, 128], F32)
    wr_h = dt_("wr", [NKC, 128, NE], F32)
    gcol_h = dt_("gcol", [128, NKC], F32)
    bcol_h = dt_("bcol", [128, NKC], F32)
    cw_h = dt_("cw", [NCI, 128, 4], F32)
    cb_h = dt_("cb", [128, NCI], F32)
    dtb_h = dt_("dtb", [128, NCI], F32)
    alog_h = dt_("alog", [NCI, 128, DS], F32)
    dcol_h = dt_("dcol", [128, NCI], F32)
    b1c_h = dt_("b1c", [NE, 128, 32], F32)
    b2_h = dt_("b2", [NE, D], F32)
    glub_h = dt_("glub", [128, 16], F32)
    mask_h = dt_("coremask", [8, 1], F32)
    out_h = dt_("out", [D, T], F32, kind="ExternalOutput")

    u_dram = scratch("u_dram", [DI, T], BF16)
    z_dram = scratch("z_dram", [DI, T], BF16)
    yg_dram = scratch("yg_dram", [DI, T], BF16)
    yacc_dram = scratch("yacc_dram", [DI, T], F32)
    s_dram = scratch("s_dram", [D, T], F32)
    bc_dram = scratch("bc_dram", [32, T], BF16)
    stat_dram = scratch("stat_dram", [8, T], F32)
    rt_dram = scratch("rt_dram", [32, T], F32)
    w8_dram = scratch("w8_dram", [8, T], F32)
    xnbf_dram = scratch("xnbf_dram", [D, T], BF16)
    h_dram = scratch("h_dram", [NE * 4096, T], BF16)
    dt_dram = scratch("dt_dram", [DI, T], BF16)
    dtu_dram = scratch("dtu_dram", [DI, T], BF16)
    cc_in = scratch("cc_in", [16, 1], F32)
    cc_out = scratch("cc_out", [128, 1], F32)

    es = ExitStack()
    sb = lambda name, shape, dtype: es.enter_context(nc.sbuf_tensor(name, shape, dtype))
    ps = lambda name, shape, dtype: es.enter_context(nc.psum_tensor(name, shape, dtype))

    XNBF = [sb(f"xnbf{i}", [128, T], BF16) for i in range(8)]
    BA = [sb(f"ba{i}", [128, T], BF16) for i in range(8)]
    FA = [sb(f"fa{i}", [128, T], F32) for i in range(8)]
    WST = [sb(f"wst{i}", [128, 1024], F32) for i in range(2)]
    WB1 = [sb(f"wb1_{i}", [128, 1024], BF16) for i in range(4)]
    WB2 = [sb(f"wb2_{i}", [128, 1024], BF16) for i in range(4)]
    WBC = sb("wbc", [128, T], F32)
    XIP = [sb(f"xip{i}", [128, T + 4], BF16) for i in range(2)]
    UBUF = [sb(f"ubuf{i}", [128, T], BF16) for i in range(2)]
    FH2 = sb("fh2", [128, T], F32)
    xdbT = FH2
    bcbf = sb("bcbf", [32, T], BF16)
    dtraw = sb("dtraw", [64, T], BF16)
    HR = [sb(f"hr{i}", [128, 1024], BF16) for i in range(2)]
    ident = sb("ident", [128, 128], F32)
    ones = sb("ones", [128, 1], F32)
    Asb = [sb(f"asb{i}", [128, DS], F32) for i in range(NCI)]
    gcol = sb("gcols", [128, NKC], F32)
    bcol = sb("bcols", [128, NKC], F32)
    cwsb = sb("cwsb", [128, 4 * NCI], F32)
    cbsb = sb("cbsb", [128, NCI], F32)
    dtbsb = sb("dtbsb", [128, NCI], F32)
    dcsb = sb("dcsb", [128, NCI], F32)
    glubsb = sb("glubsb", [128, 16], F32)
    wrsb = sb("wrsb", [128, NKC * NE], F32)
    wxpbf = [sb(f"wxpbf{i}", [128, 96], BF16) for i in range(2)]
    wdtbf = [sb(f"wdtbf{i}", [64, 128], BF16) for i in range(2)]
    b1sb = [sb(f"b1sb{i}", [128, 32], F32) for i in range(2)]
    b2f = sb("b2f", [8, D], F32)
    b2b = sb("b2b", [8, D], BF16)
    Gsb = sb("gsb", [8, 16], F32)
    msk = sb("msk", [8, 1], F32)
    offs = sb("offs", [8, 2], F32)
    mx8 = sb("mx8", [128, 8], F32)
    eqs = sb("eqs", [128, 32], F32)
    tvs = sb("tvs", [128, 2], F32)
    TP = sb("tp", [128, 128], F32)
    LG = sb("lg", [128, 128], F32)

    PA = ps("pa", [128, T], F32)
    PB = ps("pb", [128, T], F32)

    E = Emitter()

    def dma(q, out, in_, waits=(), **kw):
        sem = "SX" if q == "sync" else "SG"
        return E.add(q, lambda e: e.dma_start(out=out, in_=in_, **kw),
                     waits=waits, inc=sem, amt=16)

    def mm(out, lhsT, rhs, start, stop, waits=(), inc=False):
        return E.add("tensor",
                     lambda e: e.matmul(out, lhsT, rhs, start=start, stop=stop),
                     waits=waits, inc=("SM" if inc else None))

    def act(out, in_, func, bias=0.0, scale=1.0, waits=(), inc=True):
        return E.add("scalar",
                     lambda e: e.activation(out, in_, func, bias=bias, scale=scale),
                     waits=waits, inc=("SA" if inc else None))

    def dve(fn, waits=(), inc=True):
        return E.add("vector", fn, waits=waits, inc=("SV" if inc else None))

    def pool(fn, waits=(), inc=True):
        return E.add("gpsimd", fn, waits=waits, inc=("SP" if inc else None))

    def bcast(q, dst, src_row, waits=()):
        return dma(q, dst, src_row.partition_broadcast(128).squeeze(1), waits=waits)

    stage_free = [("SA", 0), ("SA", 0)]
    stage_g = [0]

    def stage_load(dram_apm, nrows, ncols, q="sync", extra=()):
        st = stage_g[0] % 2
        stage_g[0] += 1
        a = dram_apm.shape[0]
        d = dma(q, WST[st][0:nrows, 0:a * ncols].rearrange("p (a b) -> p a b", a=a),
                dram_apm.rearrange("a p m -> p a m"),
                waits=[stage_free[st]] + list(extra))
        return st, d

    class Ring:
        def __init__(self, ring):
            self.ring = ring
            self.slot_free = [0] * len(ring)

        def load(self, dram_apm, q="sync", extra=(), cast_eng=None):
            gi = getattr(self, "g", 0)
            self.g = gi + 1
            sl = gi % len(self.ring)
            st, d = stage_load(dram_apm, 128, 128, q=q,
                               extra=list(extra) + [("SM", self.slot_free[sl])])
            dw = [("SX" if q == "sync" else "SG", d)]
            if cast_eng == "gpsimd":
                a = pool(lambda e, sl=sl, st=st: e.tensor_copy(self.ring[sl][:], WST[st][:]),
                         waits=dw)
                stage_free[st] = ("SP", a)
            else:
                a = act(self.ring[sl][:], WST[st][:], AF.Copy, waits=dw)
                stage_free[st] = ("SA", a)
            return sl, a

        def consumed(self, sl):
            self.slot_free[sl] = E.c("SM")

    ring1 = Ring(WB1)
    ring2 = Ring(WB2)

    pool(lambda e: e.memset(ident[:], 0.0), inc=False)
    ident_v = pool(lambda e: e.affine_select(out=ident[:], in_=ident[:],
                                             compare_op=ALU.not_equal, fill=1.0, base=0,
                                             pattern=[[-1, 128]], channel_multiplier=1))
    ones_v = dve(lambda e: e.memset(ones[:], 1.0))
    dma("gpsimd", gcol[:], gcol_h[:])
    dma("gpsimd", bcol[:], bcol_h[:])
    for c in range(NCI):
        dma("gpsimd", cwsb[:, c * 4:(c + 1) * 4], cw_h[c])
    dma("gpsimd", cbsb[:], cb_h[:])
    dma("gpsimd", dtbsb[:], dtb_h[:])
    dma("gpsimd", dcsb[:], dcol_h[:])
    dma("gpsimd", glubsb[:], glub_h[:])
    dma("gpsimd", msk[:], mask_h[:])
    b2d = dma("gpsimd", b2f[:], b2_h[:])
    for kc in range(NKC):
        dma("gpsimd", wrsb[:, kc * 8:(kc + 1) * 8], wr_h[kc])
    for c in range(NCI):
        v = dma("gpsimd", Asb[c][:], alog_h[c])
        act(Asb[c][:], Asb[c][:], AF.Exp, waits=[("SG", v)])
        dve(lambda e, t=Asb[c]: e.tensor_scalar_mul(t[:], t[:], -1.0),
            waits=[("SA", E.c("SA"))])
    b2cast = dve(lambda e: e.tensor_copy(b2b[:], b2f[:]), waits=[("SG", b2d)])
    consts_g = E.c("SG")

    xld = [0] * NKC
    xld[0] = dma("sync", FA[0][:], x_h[0:128, :])
    xld[1] = dma("sync", FA[1][:], x_h[128:256, :])
    sumx_mm = [0] * NKC
    sqm_mm = [0] * NKC
    sq_a = [0] * NKC
    for kc in range(NKC):
        xb = FA[kc % 2]
        for tt in range(NTT):
            mm(PB[0:1, tt * 512:(tt + 1) * 512], ones[:], xb[:, tt * 512:(tt + 1) * 512],
               start=(kc == 0), stop=(kc == NKC - 1),
               waits=[("SX", xld[kc]), ("SV", ones_v)], inc=(tt == NTT - 1))
        sumx_mm[kc] = E.c("SM")
        sq = act(FA[2][:], xb[:], AF.Square,
                 waits=[("SX", xld[kc])] + ([("SM", sqm_mm[kc - 1])] if kc >= 1 else []))
        sq_a[kc] = sq
        for tt in range(NTT):
            mm(PA[0:1, tt * 512:(tt + 1) * 512], ones[:], FA[2][:, tt * 512:(tt + 1) * 512],
               start=(kc == 0), stop=(kc == NKC - 1),
               waits=[("SA", sq)], inc=(tt == NTT - 1))
        sqm_mm[kc] = E.c("SM")
        if kc + 2 < NKC:
            xld[kc + 2] = dma("sync", xb[:], x_h[(kc + 2) * 128:(kc + 3) * 128, :],
                              waits=[("SA", sq), ("SM", sumx_mm[kc])])
    mm_done = E.c("SM")
    mu1 = FA[3][0:1, :]
    rstd1 = FA[4][0:1, :]
    vtmp = FA[5][0:1, :]
    act(mu1, PB[0:1, :], AF.Copy, scale=1.0 / D, waits=[("SM", mm_done)])
    act(vtmp, PA[0:1, :], AF.Copy, scale=1.0 / D)
    a_musq = act(FA[2][0:1, :], mu1, AF.Square)
    dve(lambda e: e.tensor_sub(vtmp, vtmp, FA[2][0:1, :]), waits=[("SA", a_musq)])
    v_var = dve(lambda e: e.tensor_scalar_add(vtmp, vtmp, EPS))
    act(vtmp, vtmp, AF.Ln, waits=[("SV", v_var)])
    a_r1 = act(rstd1, vtmp, AF.Exp, scale=-0.5)
    d_st1 = dma("gpsimd", stat_dram[0:1, :], mu1, waits=[("SA", E.c("SA")), ("SG", consts_g)])
    d_st2 = dma("gpsimd", stat_dram[1:2, :], rstd1, waits=[("SA", a_r1)])
    mu1b, rstd1b = FA[3], FA[4]
    d_mu1b = bcast("gpsimd", mu1b[:], stat_dram[0:1, :], waits=[("SG", d_st1)])
    d_rs1b = bcast("gpsimd", rstd1b[:], stat_dram[1:2, :], waits=[("SG", d_st2)])

    if stop < 2:
        _fin(E, nc, es)
        return nc

    xld = [0] * NKC
    xld[0] = dma("sync", FA[0][:], x_h[0:128, :],
                 waits=[("SA", sq_a[7]), ("SM", mm_done)])
    xld[1] = dma("sync", FA[1][:], x_h[128:256, :])
    sxn = xdbT[0:1, :]
    sxq = xdbT[32:33, :]
    sq2_a = [0] * NKC
    v_sx = v_sq = v_lg = 0
    xn_mm = [0] * NKC
    for kc in range(NKC):
        xb = FA[kc % 2]
        xn = FA[5]
        w_ = [("SX", xld[kc]), ("SG", max(d_mu1b, d_rs1b))]
        if kc >= 1:
            w_ += [("SM", xn_mm[kc - 1]), ("SA", sq2_a[kc - 1])]
        v1 = dve(lambda e, xb=xb: e.tensor_sub(xn[:], xb[:], mu1b[:]), waits=w_)
        v2 = dve(lambda e: e.tensor_mul(xn[:], xn[:], rstd1b[:]))
        act(XNBF[kc][:], xn[:], AF.Copy, waits=[("SV", v2)])
        for tt in range(NTT):
            mm(PB[0:1, tt * 512:(tt + 1) * 512], ones[:], xn[:, tt * 512:(tt + 1) * 512],
               start=True, stop=True, waits=[("SV", v2), ("SV", v_sx)],
               inc=(tt == NTT - 1))
        mmx = E.c("SM")
        if kc == 0:
            v_sx = dve(lambda e: e.tensor_copy(sxn, PB[0:1, :]), waits=[("SM", mmx)])
        else:
            v_sx = dve(lambda e: e.tensor_add(sxn, sxn, PB[0:1, :]), waits=[("SM", mmx)])
        sq2 = act(FA[2][:], xn[:], AF.Square,
                  waits=[("SV", v2)] + ([("SM", xn_mm[kc - 1])] if kc >= 1 else [("SM", mm_done)]))
        sq2_a[kc] = sq2
        for tt in range(NTT):
            mm(PA[0:1, tt * 512:(tt + 1) * 512], ones[:], FA[2][:, tt * 512:(tt + 1) * 512],
               start=True, stop=True,
               waits=[("SA", sq2), ("SV", v_sq), ("SV", v_lg)],
               inc=(tt == NTT - 1))
        mmq = E.c("SM")
        if kc == 0:
            v_sq = dve(lambda e: e.tensor_copy(sxq, PA[0:1, :]), waits=[("SM", mmq)])
        else:
            v_sq = dve(lambda e: e.tensor_add(sxq, sxq, PA[0:1, :]), waits=[("SM", mmq)])
        for ts_ in range(16):
            mm(PA[:, 512 + ts_ * 8:512 + (ts_ + 1) * 8],
               xn[:, ts_ * 128:(ts_ + 1) * 128],
               wrsb[:, kc * 8:(kc + 1) * 8],
               start=True, stop=True,
               waits=[("SG", consts_g), ("SV", v_sq)],
               inc=(ts_ == 15))
        mmr = E.c("SM")
        if kc == 0:
            v_lg = dve(lambda e: e.tensor_copy(LG[:], PA[:, 512:640]), waits=[("SM", mmr)])
        else:
            v_lg = dve(lambda e: e.tensor_add(LG[:], LG[:], PA[:, 512:640]), waits=[("SM", mmr)])
        xn_mm[kc] = E.c("SM")
        if kc + 2 < NKC:
            xld[kc + 2] = dma("sync", xb[:], x_h[(kc + 2) * 128:(kc + 3) * 128, :],
                              waits=[("SA", sq2), ("SM", xn_mm[kc])])
    rt_mm = E.c("SM")
    mu2 = FA[2][0:1, :]
    rstd2 = FA[5][0:1, :]
    act(mu2, sxn, AF.Copy, scale=1.0 / D, waits=[("SV", v_lg)])
    act(rstd2, sxq, AF.Copy, scale=1.0 / D)
    a_m2 = act(FA[0][0:1, :], mu2, AF.Square)
    dve(lambda e: e.tensor_sub(rstd2, rstd2, FA[0][0:1, :]), waits=[("SA", a_m2)])
    v_v2 = dve(lambda e: e.tensor_scalar_add(rstd2, rstd2, EPS))
    act(rstd2, rstd2, AF.Ln, waits=[("SV", v_v2)])
    a_r2 = act(rstd2, rstd2, AF.Exp, scale=-0.5)
    d_s3 = dma("gpsimd", stat_dram[2:3, :], mu2, waits=[("SA", a_r2)])
    d_s4 = dma("gpsimd", stat_dram[3:4, :], rstd2, waits=[("SA", a_r2)])

    tp_guard = [0, 0]
    last_tr = 0
    ev_q = 0
    for tk in range(16):
        lg = LG[:, tk * 8:(tk + 1) * 8]
        vmax = dve(lambda e, lg=lg: e.max(out=mx8[:], in_=lg), waits=[("SV", v_lg)])
        dve(lambda e, lg=lg: e.tensor_scalar(eqs[:, 0:8], lg, mx8[:, 0:1], None, ALU.is_equal),
            waits=[("SV", vmax)])
        dve(lambda e, lg=lg: e.tensor_scalar(eqs[:, 8:16], lg, mx8[:, 1:2], None, ALU.is_equal))
        vt = dve(lambda e: e.tensor_sub(tvs[:, 0:1], mx8[:, 0:1], mx8[:, 1:2]))
        a_t1 = act(tvs[:, 0:1], tvs[:, 0:1], AF.Sigmoid, waits=[("SV", vt)])
        a_t2 = act(tvs[:, 1:2], tvs[:, 0:1], AF.Copy, scale=-1.0, bias=1.0)
        dve(lambda e: e.tensor_scalar_mul(eqs[:, 16:24], eqs[:, 0:8], tvs[:, 0:1]),
            waits=[("SA", a_t2)])
        dve(lambda e: e.tensor_scalar_mul(eqs[:, 24:32], eqs[:, 8:16], tvs[:, 1:2]))
        dve(lambda e: e.tensor_copy(TP[:, 0:8], eqs[:, 0:8]),
            waits=[("SM", last_tr)])
        dve(lambda e: e.tensor_copy(TP[:, 32:40], eqs[:, 8:16]))
        dve(lambda e: e.tensor_copy(TP[:, 64:72], eqs[:, 16:24]))
        p4 = dve(lambda e: e.tensor_copy(TP[:, 96:104], eqs[:, 24:32]))
        tpp = PB[:, 256 + (tk % 2) * 512:384 + (tk % 2) * 512]
        last_tr = E.add("tensor", lambda e, tpp=tpp: e.transpose(tpp, TP[:], ident[:]),
                        waits=[("SV", p4), ("SA", E.c("SA")), ("SV", tp_guard[tk % 2]),
                               ("SP", ident_v)],
                        inc="SM")
        dve(lambda e, tpp=tpp, tk=tk: e.tensor_copy(FA[6][0:8, tk * 128:(tk + 1) * 128], tpp[0:8, :]),
            waits=[("SM", last_tr)])
        dve(lambda e, tpp=tpp, tk=tk: e.tensor_copy(FA[6][32:40, tk * 128:(tk + 1) * 128], tpp[32:40, :]))
        dve(lambda e, tpp=tpp, tk=tk: e.tensor_copy(FA[6][64:72, tk * 128:(tk + 1) * 128], tpp[64:72, :]))
        ev_q = dve(lambda e, tpp=tpp, tk=tk: e.tensor_copy(FA[6][96:104, tk * 128:(tk + 1) * 128], tpp[96:104, :]))
        tp_guard[tk % 2] = ev_q
    r1 = dve(lambda e: e.tensor_tensor_scan(FA[7][0:8, :], FA[6][0:8, :], FA[6][0:8, :],
                                            0.0, ALU.add, ALU.bypass), waits=[("SV", ev_q)])
    r2 = dve(lambda e: e.tensor_tensor_scan(FA[7][32:40, :], FA[6][32:40, :], FA[6][32:40, :],
                                            0.0, ALU.add, ALU.bypass))
    d_t1 = dma("gpsimd", cc_in[0:8, :], FA[7][0:8, T - 1:T], waits=[("SV", r1)])
    d_t2 = dma("gpsimd", cc_in[8:16, :], FA[7][32:40, T - 1:T], waits=[("SV", r2)])
    E.add("gpsimd",
          lambda e: e.collective_compute(
              "AllGather", ALU.bypass, replica_groups=[list(range(8))],
              ins=[cc_in.ap().opt()], outs=[cc_out.ap().opt()]),
          waits=[("SG", max(d_t1, d_t2))], inc="SC")
    dma("sync", rt_dram[0:8, :], FA[6][64:72, :], waits=[("SV", r2)])
    dma("sync", rt_dram[8:16, :], FA[6][96:104, :])
    dma("sync", rt_dram[16:24, :], FA[7][0:8, :])
    dma("sync", rt_dram[24:32, :], FA[7][32:40, :])
    rt_parked = E.c("SX")

    if stop < 4:
        _fin(E, nc, es)
        return nc

    mu2b, rstd2b = FA[3], FA[4]
    d_m2b = bcast("gpsimd", mu2b[:], stat_dram[2:3, :], waits=[("SG", d_s3)])
    d_r2b = bcast("gpsimd", rstd2b[:], stat_dram[3:4, :], waits=[("SG", d_s4)])
    hln_a = 0
    for kc in range(NKC):
        t = FA[5]
        v1 = dve(lambda e, kc=kc: e.tensor_sub(t[:], XNBF[kc][:], mu2b[:]),
                 waits=[("SG", max(d_m2b, d_r2b)), ("SX", rt_parked)])
        dve(lambda e: e.tensor_mul(t[:], t[:], rstd2b[:]))
        hln_a = dve(lambda e, kc=kc: e.tensor_scalar(
            BA[kc][:], t[:], gcol[:, kc:kc + 1], bcol[:, kc:kc + 1],
            ALU.mult, ALU.add))

    if stop < 5:
        _fin(E, nc, es)
        return nc

    pgA = [0, 0]
    ordered_mc = list(range(16, 32)) + list(range(16))
    conv_v = 0
    uc_parked = [0] * NCI
    zc_parked = [0] * NCI
    memset_done = {}
    for idx, mc in enumerate(ordered_mc):
        sl, aw = ring1.load(win_h[mc])
        P_ = PA if idx % 2 == 0 else PB
        for kc in range(NKC):
            for tt in range(NTT):
                mm(P_[:, tt * 512:(tt + 1) * 512],
                   WB1[sl][:, kc * 128:(kc + 1) * 128],
                   BA[kc][:, tt * 512:(tt + 1) * 512],
                   start=(kc == 0), stop=(kc == NKC - 1),
                   waits=[("SA", aw), ("SV", hln_a), ("SA", pgA[idx % 2])],
                   inc=(kc == NKC - 1 and tt == NTT - 1))
        gmm = E.c("SM")
        ring1.consumed(sl)
        if mc >= 16:
            c = mc - 16
            zb = XIP[idx % 2]
            ev = act(zb[:, 0:T], P_[:], AF.Silu,
                     waits=[("SM", gmm), ("SX", zc_parked[c - 2] if c >= 2 else 0)])
            zc_parked[c] = dma("sync", z_dram[c * 128:(c + 1) * 128, :], zb[:, 0:T],
                               waits=[("SA", ev)])
            pgA[idx % 2] = ev
        else:
            c = mc
            xp = XIP[idx % 2]
            if c in (0, 1):
                memset_done[c] = dve(lambda e, xp=xp: e.memset(xp[:, 0:4], 0.0),
                                     waits=[("SX", zc_parked[14 + c])])
            ev = act(xp[:, 4:4 + T], P_[:], AF.Copy,
                     waits=[("SM", gmm), ("SV", conv_v),
                            ("SV", memset_done.get(c, 0)),
                            ("SX", zc_parked[14 + c] if c in (0, 1) else 0)])
            pgA[idx % 2] = ev
            t0, t1 = FA[0], FA[1]
            dve(lambda e, c=c, xp=xp: e.tensor_scalar_mul(
                t0[:], xp[:, 1:1 + T], cwsb[:, c * 4:c * 4 + 1]),
                waits=[("SA", ev), ("SG", consts_g)])
            dve(lambda e, c=c, xp=xp: e.scalar_tensor_tensor(
                t1[:], xp[:, 2:2 + T], cwsb[:, c * 4 + 1:c * 4 + 2], t0[:], ALU.mult, ALU.add))
            dve(lambda e, c=c, xp=xp: e.scalar_tensor_tensor(
                t0[:], xp[:, 3:3 + T], cwsb[:, c * 4 + 2:c * 4 + 3], t1[:], ALU.mult, ALU.add))
            conv_v = dve(lambda e, c=c, xp=xp: e.scalar_tensor_tensor(
                t1[:], xp[:, 4:4 + T], cwsb[:, c * 4 + 3:c * 4 + 4], t0[:], ALU.mult, ALU.add))
            uev = act(UBUF[c % 2][:], t1[:], AF.Silu, bias=cbsb[:, c:c + 1],
                      waits=[("SV", conv_v), ("SX", uc_parked[c - 2] if c >= 2 else 0)])
            uc_parked[c] = dma("sync", u_dram[c * 128:(c + 1) * 128, :], UBUF[c % 2][:],
                               waits=[("SA", uev)])

    if stop < 6:
        _fin(E, nc, es)
        return nc

    uld = [0] * NCI
    uld[0] = dma("sync", UBUF[0][:], u_dram[0:128, :], waits=[("SX", uc_parked[15])])
    uld[1] = dma("sync", UBUF[1][:], u_dram[128:256, :])
    wx_cast = [0, 0]
    xp_mm_c = [0] * NCI
    for c in range(NCI):
        wsl = c % 2
        st, dwx = stage_load(wxp_h[c:c + 1], 128, 96, q="gpsimd")
        aw = act(wxpbf[wsl][:], WST[st][0:128, 0:96], AF.Copy,
                 waits=[("SG", dwx)] + ([("SM", xp_mm_c[c - 2])] if c >= 2 else []))
        stage_free[st] = ("SA", aw)
        wx_cast[wsl] = aw
        for tt in range(NTT):
            mm(PA[0:96, tt * 512:(tt + 1) * 512], wxpbf[wsl][:],
               UBUF[c % 2][:, tt * 512:(tt + 1) * 512],
               start=(c == 0), stop=(c == NCI - 1),
               waits=[("SA", aw), ("SX", uld[c]), ("SA", pgA[0]), ("SA", pgA[1])],
               inc=(tt == NTT - 1))
        xp_mm_c[c] = E.c("SM")
        if c + 2 < NCI:
            uld[c + 2] = dma("sync", UBUF[c % 2][:], u_dram[(c + 2) * 128:(c + 3) * 128, :],
                             waits=[("SM", xp_mm_c[c])])
    xp_mm = E.c("SM")
    ev1 = act(xdbT[0:96, :], PA[0:96, :], AF.Copy, waits=[("SM", xp_mm)])
    ev2 = act(dtraw[:], xdbT[0:64, :], AF.Copy)
    ev3 = act(bcbf[:], xdbT[64:96, :], AF.Copy)
    d_b = dma("sync", bc_dram[:, :], bcbf[:], waits=[("SA", ev3)])
    bc_parked = d_b

    if stop < 7:
        _fin(E, nc, es)
        return nc

    l1 = dma("gpsimd", FA[0][0:8, :], rt_dram[0:8, :],
             waits=[("SV", E.c("SV")), ("SA", E.c("SA"))])
    dma("gpsimd", FA[0][32:40, :], rt_dram[8:16, :])
    dma("gpsimd", FA[0][64:72, :], rt_dram[16:24, :])
    l4 = dma("gpsimd", FA[0][96:104, :], rt_dram[24:32, :])
    lg_ = dma("gpsimd", Gsb[:], cc_out.ap().rearrange("(r k) o -> r (k o)", r=8),
              waits=[("SC", 1)])
    mm(PB[0:8, 0:1], Gsb[:, 0:8], msk[:], True, True,
       waits=[("SG", lg_), ("SA", ev3), ("SM", xp_mm)])
    mm(PB[0:8, 512:513], Gsb[:, 8:16], msk[:], True, True, inc=True)
    off_mm = E.c("SM")
    dve(lambda e: e.tensor_copy(offs[:, 0:1], PB[0:8, 0:1]), waits=[("SM", off_mm)])
    vo2 = dve(lambda e: e.tensor_copy(offs[:, 1:2], PB[0:8, 512:513]))
    dve(lambda e: e.tensor_scalar(FA[1][0:8, :], FA[0][64:72, :], offs[:, 0:1], None, ALU.add),
        waits=[("SV", vo2), ("SG", l4)])
    dve(lambda e: e.tensor_scalar(FA[1][32:40, :], FA[0][96:104, :], offs[:, 1:2], None, ALU.add))
    dve(lambda e: e.tensor_scalar(FA[1][0:8, :], FA[1][0:8, :], CAP + 0.5, None, ALU.is_le))
    dve(lambda e: e.tensor_scalar(FA[1][32:40, :], FA[1][32:40, :], CAP + 0.5, None, ALU.is_le))
    dve(lambda e: e.tensor_mul(FA[1][0:8, :], FA[1][0:8, :], FA[0][0:8, :]))
    dve(lambda e: e.tensor_mul(FA[1][32:40, :], FA[1][32:40, :], FA[0][32:40, :]))
    dve(lambda e: e.tensor_copy(FA[2][0:8, :], FA[1][32:40, :]))
    w8v = dve(lambda e: e.tensor_add(FA[1][0:8, :], FA[1][0:8, :], FA[2][0:8, :]))
    d_w8 = dma("sync", w8_dram[:, :], FA[1][0:8, :], waits=[("SV", w8v)])
    w8bf_v = bcbf[0:8, :]
    w8c = dve(lambda e: e.tensor_copy(w8bf_v, FA[1][0:8, :]), waits=[("SX", d_b)])

    f_dt, f_dtu, f_a0, f_a1 = FA[0], FA[1], FA[2], FA[3]
    f_bb = [FA[4], WBC]
    f_hh = [FA[5], FH2]
    f_yacc = [FA[6], FA[7]]
    uld2 = [0] * NCI
    wdt_cast = [0, 0]
    dt_mm_guard = [0, 0]
    spl_par = [0, 0]
    dtp_out = [0, 0]
    dtc_a = [0, 0]
    dtc_v = [0, 0]
    uld2[0] = dma("sync", UBUF[0][:], u_dram[0:128, :], waits=[("SX", uc_parked[15])])
    uld2[1] = dma("sync", UBUF[1][:], u_dram[128:256, :])
    for c in range(NCI):
        wsl = c % 2
        st, dwt = stage_load(wdt_h[c:c + 1], 64, 128, q="gpsimd",
                             extra=[("SA", wdt_cast[wsl])])
        awt = act(wdtbf[wsl][:], WST[st][0:64, 0:128], AF.Copy,
                  waits=[("SG", dwt), ("SM", dt_mm_guard[wsl])])
        stage_free[st] = ("SA", awt)
        wdt_cast[wsl] = awt
        Pd = PA if wsl == 0 else PB
        for tt in range(NTT):
            mm(Pd[:, tt * 512:(tt + 1) * 512], wdtbf[wsl][:],
               dtraw[:, tt * 512:(tt + 1) * 512], start=True, stop=True,
               waits=[("SA", awt), ("SA", ev2), ("SA", spl_par[wsl]),
                      ("SM", xp_mm), ("SV", w8c)],
               inc=(tt == NTT - 1))
        dt_mm = E.c("SM")
        dt_mm_guard[wsl] = dt_mm
        sp_w = [("SM", dt_mm), ("SV", dtc_v[wsl]), ("SA", dtc_a[wsl])]
        act(FA[4][:], Pd[:], AF.Relu, bias=dtbsb[:, c:c + 1], waits=sp_w)
        act(f_dtu[:], Pd[:], AF.Abs, bias=dtbsb[:, c:c + 1])
        act(f_dtu[:], f_dtu[:], AF.Exp, scale=-1.0)
        spl = act(f_dtu[:], f_dtu[:], AF.Ln, bias=1.0)
        spl_par[wsl] = spl
        vdt = dve(lambda e: e.tensor_add(f_dt[:], FA[4][:], f_dtu[:]),
                  waits=[("SA", spl)])
        vdtu = dve(lambda e, c=c: e.tensor_mul(f_dtu[:], f_dt[:], UBUF[c % 2][:]),
                   waits=[("SX", uld2[c])])
        if c + 2 < NCI:
            uld2[c + 2] = dma("sync", UBUF[c % 2][:], u_dram[(c + 2) * 128:(c + 3) * 128, :],
                              waits=[("SV", vdtu)])
        ca = act(BA[wsl][:], f_dt[:], AF.Copy,
                 waits=[("SV", vdt), ("SX", dtp_out[wsl])])
        cv = dve(lambda e, wsl=wsl: e.tensor_copy(BA[2 + wsl][:], f_dtu[:]),
                 waits=[("SX", dtp_out[wsl])])
        dtc_a[wsl] = ca
        dtc_v[wsl] = cv
        dma("sync", dt_dram[c * 128:(c + 1) * 128, :], BA[wsl][:], waits=[("SA", ca)])
        dtp_out[wsl] = dma("sync", dtu_dram[c * 128:(c + 1) * 128, :], BA[2 + wsl][:],
                           waits=[("SV", cv)])
    dtp_done = E.c("SX")
    spl_done = max(spl_par)

    dtbit = f_dt[:].bitcast(BF16)
    dtubit = f_dtu[:].bitcast(BF16)
    bbit = FA[4][:].bitcast(BF16)

    HP = [PA[:, 0:1024], PA[:, 1024:2048], PB[:, 0:1024], PB[:, 1024:2048]]
    w1s = {"g": 0, "cur": None}
    hev_at = {}
    w1_gmm = {}
    hr_dma = [0, 0]
    b1_ld = [0, 0]
    G1 = NE * 4 * 8 * 2

    def w1_evac(g):
        e_, fs, mc, th = g // 64, (g // 16) % 4, (g // 2) % 8, g % 2
        pp = HP[g % 4]
        hv = act(HR[g % 2][:], pp, AF.Gelu,
                 bias=b1sb[e_ % 2][:, fs * 8 + mc:fs * 8 + mc + 1],
                 waits=[("SM", w1_gmm[g]), ("SG", b1_ld[e_ % 2]), ("SX", hr_dma[g % 2])])
        hev_at[g] = hv
        row = (e_ * 32 + fs * 8 + mc) * 128
        hr_dma[g % 2] = dma("sync", h_dram[row:row + 128, th * 1024:(th + 1) * 1024],
                            HR[g % 2][:], waits=[("SA", hv)])

    def w1_feed(kq):
        for _ in range(kq):
            g = w1s["g"]
            if g >= G1:
                return
            w1s["g"] = g + 1
            e_, fs, mc, th = g // 64, (g // 16) % 4, (g // 2) % 8, g % 2
            if th == 0:
                if mc == 0 and fs == 0:
                    b1_ld[e_ % 2] = dma("gpsimd", b1sb[e_ % 2][:], b1c_h[e_],
                                        waits=[("SA", hev_at.get(g - 3, 0))])
                w1s["cur"] = ring1.load(w1_h[e_, fs, mc], cast_eng="gpsimd")
            sl, aw = w1s["cur"]
            pp = HP[g % 4]
            for kc in range(NKC):
                for t2 in range(2):
                    mm(pp[:, t2 * 512:(t2 + 1) * 512],
                       WB1[sl][:, kc * 128:(kc + 1) * 128],
                       XNBF[kc][:, th * 1024 + t2 * 512:th * 1024 + (t2 + 1) * 512],
                       start=(kc == 0), stop=(kc == NKC - 1),
                       waits=[("SP", aw), ("SA", hev_at.get(g - 4, 0)),
                              ("SG", b1_ld[e_ % 2]), ("SA", spl_done)],
                       inc=(kc == NKC - 1 and t2 == 1))
            w1_gmm[g] = E.c("SM")
            if th == 1:
                ring1.consumed(sl)
            if g >= 2:
                w1_evac(g - 2)
            if g == G1 - 1:
                w1_evac(g - 1)
                w1_evac(g)

    acc_cons = [[0, 0], [0, 0]]
    dtl = [0, 0]
    dtul = [0, 0]
    zld = [0] * NCI
    ygout = [0] * NCI
    last_scan = 0
    scan_m2 = 0
    scan_at = {}
    pend = None
    last_dv = 0
    a_cnt = [0, 0]
    bb_ld = [0] * 4
    cc_ld = [0] * 4

    def flush_pending():
        nonlocal pend, last_dv
        if pend is None:
            return
        (pit, pc, pq, ph, pnn, yw0) = pend
        dve(lambda e, ph=ph, pnn=pnn: e.tensor_mul(ph[:], ph[:], BA[4 + pnn][:]),
            waits=[("SG", cc_ld[pnn])])
        if pq == 0 and pnn == 0:
            last_dv = dve(lambda e, pc=pc, ph=ph: e.tensor_copy(
                f_yacc[pc % 2][:], ph[:]), waits=yw0)
        else:
            last_dv = dve(lambda e, pc=pc, ph=ph: e.tensor_add(
                f_yacc[pc % 2][:], f_yacc[pc % 2][:], ph[:]), waits=yw0)
        pend = None

    yacc_out = [[0] * NCI for _ in range(4)]
    yacc_pref = [0] * NCI
    for q in range(4):
        for nn in range(4):
            n = q * 4 + nn
            bb_ld[nn] = bcast("gpsimd", BA[nn][:], bc_dram[n:n + 1, :],
                              waits=[("SX", bc_parked), ("SV", last_dv),
                                     ("SA", dtc_a[nn % 2] if q == 0 else 0),
                                     ("SX", dtp_done if q == 0 else 0)])
            cc_ld[nn] = bcast("gpsimd", BA[4 + nn][:], bc_dram[16 + n:17 + n, :],
                              waits=[("SV", last_dv),
                                     ("SV", dtc_v[nn % 2] if q == 0 else 0),
                                     ("SX", dtp_done if q == 0 else 0)])
        for c in range(NCI):
            sl_ = c % 2
            if q == 0 and c == 0:
                for pc_ in (0, 1):
                    dtl[pc_] = dma("sync", dtbit[:, pc_ * 2048:(pc_ + 1) * 2048],
                                   dt_dram[pc_ * 128:(pc_ + 1) * 128, :],
                                   waits=[("SX", dtp_done)])
                    dtul[pc_] = dma("sync", dtubit[:, pc_ * 2048:(pc_ + 1) * 2048],
                                    dtu_dram[pc_ * 128:(pc_ + 1) * 128, :],
                                    waits=[("SX", dtp_done)])
            nc_, nq_ = (c + 1, q) if c + 1 < NCI else (0, q + 1)
            if nq_ < 4 and not (q == 0 and c == 0):
                nsl = nc_ % 2
                dtl[nsl] = dma("sync", dtbit[:, nsl * 2048:(nsl + 1) * 2048],
                               dt_dram[nc_ * 128:(nc_ + 1) * 128, :],
                               waits=[("SA", acc_cons[0][nsl])])
                dtul[nsl] = dma("sync", dtubit[:, nsl * 2048:(nsl + 1) * 2048],
                                dtu_dram[nc_ * 128:(nc_ + 1) * 128, :],
                                waits=[("SV", acc_cons[1][nsl])])
            if q == 3:
                uld2[c] = dma("sync", UBUF[c % 2][:], u_dram[c * 128:(c + 1) * 128, :],
                              waits=[("SX", ygout[c - 2] if c >= 2 else 0),
                                     ("SV", last_dv if c >= 2 else 0)])
                zld[c] = dma("sync", XIP[c % 2][:, 0:T], z_dram[c * 128:(c + 1) * 128, :],
                             waits=[("SV", last_dv)])
            if q > 0:
                yacc_in = yacc_pref[c]
            else:
                yacc_in = 0
            if nq_ > 0 and nq_ < 4:
                sg2_ = yacc_out[nq_][nc_ - 2] if nc_ >= 2 else yacc_out[nq_ - 1][NCI - 2 + nc_]
                yacc_pref[nc_] = dma("sync", f_yacc[nc_ % 2][:],
                                     yacc_dram[nc_ * 128:(nc_ + 1) * 128, :],
                                     waits=[("SX", yacc_out[nq_ - 1][nc_]), ("SX", sg2_),
                                            ("SV", last_dv)])
            for nn in range(4):
                n = q * 4 + nn
                it = q * NCI * 4 + c * 4 + nn
                ab = [f_a0, f_a1][it % 2]
                fb = bbit[:, (it % 2) * 2048:(it % 2 + 1) * 2048]
                fh = f_hh[it % 2]
                aa = act(ab[:], dtbit[:, sl_ * 2048:(sl_ + 1) * 2048],
                         AF.Exp, scale=Asb[c][:, n:n + 1],
                         waits=[("SX", dtl[sl_]), ("SV", scan_m2)])
                a_cnt[it % 2] = aa
                acc_cons[0][sl_] = aa
                bv = dve(lambda e, nn=nn, fb=fb, sl_=sl_: e.tensor_mul(
                    fb, dtubit[:, sl_ * 2048:(sl_ + 1) * 2048], BA[nn][:]),
                    waits=[("SG", bb_ld[nn]), ("SX", dtul[sl_])])
                acc_cons[1][sl_] = bv
                sc = dve(lambda e, ab=ab, fb=fb, fh=fh: e.tensor_tensor_scan(
                    fh[:], ab[:], fb, 0.0, ALU.mult, ALU.add),
                    waits=[("SA", aa)])
                scan_at[it] = sc
                scan_m2 = last_scan
                last_scan = sc
                yw0 = []
                if nn == 0:
                    if q == 0:
                        yw0 = [("SX", yacc_out[0][c - 2] if c >= 2 else 0)]
                    else:
                        yw0 = [("SX", yacc_in)]
                npend = (it, c, q, fh, nn, yw0)
                flush_pending()
                pend = npend
                w1_feed(2)
            flush_pending()
            if q < 3:
                yacc_out[q][c] = dma("sync", yacc_dram[c * 128:(c + 1) * 128, :],
                                     f_yacc[c % 2][:], waits=[("SV", last_dv)])
            else:
                au = act(FA[4][:], UBUF[c % 2][:], AF.Copy, scale=dcsb[:, c:c + 1],
                         waits=[("SV", last_dv), ("SX", uld2[c])])
                dve(lambda e, c=c: e.tensor_add(f_yacc[c % 2][:], f_yacc[c % 2][:], FA[4][:]),
                    waits=[("SA", au)])
                v2 = dve(lambda e, c=c: e.tensor_mul(
                    UBUF[c % 2][:], f_yacc[c % 2][:], XIP[c % 2][:, 0:T]),
                    waits=[("SX", zld[c])])
                last_dv = v2
                ygout[c] = dma("sync", yg_dram[c * 128:(c + 1) * 128, :], UBUF[c % 2][:],
                               waits=[("SV", v2)])
    w1_feed(G1)
    w1_last_hev = hev_at[G1 - 1]
    w1_h_parked = hr_dma[0] if hr_dma[0] > hr_dma[1] else hr_dma[1]

    if stop < 8:
        _fin(E, nc, es)
        return nc

    op_ev = 0
    ygmm = {}
    hg_first = [("SA", w1_last_hev), ("SP", E.c("SP")), ("SV", last_dv)]
    for tt in range(NTT):
        for kc in range(NCI):
            prev = (tt, kc - 2) if kc >= 2 else (tt - 1, NCI - 2 + kc)
            ygl = dma("sync", UBUF[kc % 2][:, 0:512],
                      yg_dram[kc * 128:(kc + 1) * 128, tt * 512:(tt + 1) * 512],
                      waits=[("SX", ygout[NCI - 1]),
                             ("SM", ygmm.get(prev, 0))])
            sl, aw = ring1.load(wout_h[kc], q="gpsimd")
            for mcq in range(8):
                pt = (PA if mcq < 4 else PB)[:, (mcq % 4) * 512:(mcq % 4 + 1) * 512]
                mm(pt, WB1[sl][:, mcq * 128:(mcq + 1) * 128], UBUF[kc % 2][:, 0:512],
                   start=(kc == 0), stop=(kc == NCI - 1),
                   waits=[("SA", aw), ("SX", ygl), ("SA", op_ev if op_ev else w1_last_hev),
                          ("SV", last_dv)],
                   inc=(kc == NCI - 1 and mcq == 7))
            ring1.consumed(sl)
            ygmm[(tt, kc)] = E.c("SM")
        gmm = E.c("SM")
        for mcq in range(8):
            pt = (PA if mcq < 4 else PB)[:, (mcq % 4) * 512:(mcq % 4 + 1) * 512]
            op_ev = act(BA[mcq][:, tt * 512:(tt + 1) * 512], pt, AF.Gelu,
                        waits=[("SM", gmm)] + (hg_first if tt == 0 else []))
    hg_done = op_ev

    if stop < 9:
        _fin(E, nc, es)
        return nc

    glu_a = 0
    glu_b_a = 0
    glu_h0 = [0] * 8
    glu_v = 0
    s_parked = 0
    for m in range(8):
        sl2, aw2 = ring1.load(wglu_h[8 + m])
        for kc in range(NKC):
            for tt in range(NTT):
                mm(PA[:, tt * 512:(tt + 1) * 512], WB1[sl2][:, kc * 128:(kc + 1) * 128],
                   BA[kc][:, tt * 512:(tt + 1) * 512],
                   start=(kc == 0), stop=(kc == NKC - 1),
                   waits=[("SA", aw2), ("SA", hg_done), ("SA", glu_a)],
                   inc=(kc == NKC - 1 and tt == NTT - 1))
        g2mm = E.c("SM")
        ring1.consumed(sl2)
        sg2 = act(FA[0][:], PA[:], AF.Sigmoid, bias=glubsb[:, 8 + m:9 + m],
                  waits=[("SM", g2mm), ("SV", glu_v)])
        glu_a = sg2
        sl1, aw1 = ring1.load(wglu_h[m])
        for kc in range(NKC):
            for tt in range(NTT):
                mm(PB[:, tt * 512:(tt + 1) * 512], WB1[sl1][:, kc * 128:(kc + 1) * 128],
                   BA[kc][:, tt * 512:(tt + 1) * 512],
                   start=(kc == 0), stop=(kc == NKC - 1),
                   waits=[("SA", aw1), ("SV", glu_b_a)],
                   inc=(kc == NKC - 1 and tt == NTT - 1))
        g1mm = E.c("SM")
        ring1.consumed(sl1)
        a1 = dve(lambda e, m=m: e.tensor_scalar(
            FA[1][:], PB[:], glubsb[:, m:m + 1], None, ALU.add),
            waits=[("SM", g1mm), ("SV", glu_v)])
        glu_b_a = a1
        v1 = dve(lambda e: e.tensor_mul(FA[1][:], FA[1][:], FA[0][:]),
                 waits=[("SV", a1), ("SA", sg2)])
        v2 = dve(lambda e, m=m: e.tensor_add(FA[1][:], FA[1][:], XNBF[m][:]))
        glu_v = v2
        s_parked = dma("sync", s_dram[m * 128:(m + 1) * 128, :], FA[1][:],
                       waits=[("SV", v2)])
        glu_h0[m] = dma("sync", XNBF[m][:], h_dram[m * 128:m * 128 + 128, :],
                        waits=[("SV", v2), ("SX", w1_h_parked)])

    if stop < 11:
        _fin(E, nc, es)
        return nc

    PS4W = [PA[:, 0:1024], PA[:, 1024:2048], PB[:, 0:1024], PB[:, 1024:2048]]
    pgV_v = [glu_v] * 4
    wbc_ld = 0
    moe_pool = 0
    y_ev = 0
    HD = [XNBF, BA]
    hld = {}
    pool_at = {}
    sl_mm = {}
    NSL = NE * 4

    def h_loads(si):
        if si >= NSL:
            return
        e_, fs = si // 4, si % 4
        dst = HD[si % 2]
        for kc in range(8):
            if si == 0:
                hld[(si, kc)] = glu_h0[kc]
                continue
            row = (e_ * 32 + fs * 8 + kc) * 128
            w_ = [("SM", sl_mm.get(si - 2, 0)), ("SP", pool_at.get((si - 2, kc), 0))]
            if si == 1:
                w_ += [("SM", E.c("SM")), ("SA", glu_a), ("SV", glu_b_a)]
            hld[(si, kc)] = dma("sync", dst[kc][:], h_dram[row:row + 128, :],
                                waits=w_ + [("SX", w1_h_parked)])

    h_loads(0)
    for si in range(NSL):
        e_, fs = si // 4, si % 4
        dst = HD[si % 2]
        if fs == 0:
            wbc_ld = bcast("gpsimd", WBC[:], w8_dram[e_:e_ + 1, :],
                           waits=[("SX", d_w8), ("SP", moe_pool),
                                  ("SV", last_dv if e_ == 0 else 0)])
        h_loads(si + 1)
        for kc in range(8):
            moe_pool = pool(lambda e, dst=dst, kc=kc: e.tensor_mul(dst[kc][:], dst[kc][:], WBC[:]),
                            waits=[("SX", hld[(si, kc)]), ("SG", wbc_ld),
                                   ("SM", sl_mm.get(si - 2, 0))])
            pool_at[(si, kc)] = moe_pool
        for md in range(8):
            sl2, aw2 = ring2.load(w2_h[e_, fs, md])
            fold = (e_ == 0 and fs == 3)
            for th in range(2):
                g4 = (md * 2 + th) % 4
                pt4 = PS4W[g4]
                for kc in range(8):
                    for t2 in range(2):
                        mm(pt4[:, t2 * 512:(t2 + 1) * 512],
                           WB2[sl2][:, kc * 128:(kc + 1) * 128],
                           dst[kc][:, th * 1024 + t2 * 512:th * 1024 + (t2 + 1) * 512],
                           start=(kc == 0), stop=(kc == 7 and not fold),
                           waits=[("SA", aw2), ("SP", pool_at[(si, kc)]),
                                  ("SV", pgV_v[g4]),
                                  ("SA", glu_a if si == 0 else 0)],
                           inc=(kc == 7 and t2 == 1 and not fold))
                if fold:
                    for t2 in range(2):
                        mm(pt4[:, t2 * 512:(t2 + 1) * 512],
                           b2b[:, md * 128:(md + 1) * 128],
                           w8bf_v[:, th * 1024 + t2 * 512:th * 1024 + (t2 + 1) * 512],
                           start=False, stop=True,
                           waits=[("SV", w8c), ("SV", b2cast)],
                           inc=(t2 == 1))
                gmm2 = E.c("SM")
                if si == 0:
                    y_ev = dve(lambda e, md=md, th=th, pt4=pt4: e.tensor_copy(
                        FA[md][:, th * 1024:(th + 1) * 1024], pt4),
                        waits=[("SM", gmm2)])
                else:
                    y_ev = dve(lambda e, md=md, th=th, pt4=pt4: e.tensor_add(
                        FA[md][:, th * 1024:(th + 1) * 1024],
                        FA[md][:, th * 1024:(th + 1) * 1024], pt4),
                        waits=[("SM", gmm2)])
                pgV_v[g4] = y_ev
            ring2.consumed(sl2)
        sl_mm[si] = E.c("SM")
    macc_v = y_ev

    if stop < 12:
        _fin(E, nc, es)
        return nc

    out_done = 0
    ag_a = 0
    for md in range(8):
        ag = act(WBC[:], FA[md][:], AF.Silu,
                 waits=[("SV", macc_v if md == 0 else 0), ("SV", E.c("SV")),
                        ("SP", moe_pool), ("SG", wbc_ld)])
        s1 = dma("sync", WST[0][:], s_dram[md * 128:(md + 1) * 128, 0:1024],
                 waits=[("SX", s_parked), stage_free[0],
                        ("SV", E.c("SV") if md >= 1 else 0)])
        s2 = dma("sync", WST[1][:], s_dram[md * 128:(md + 1) * 128, 1024:2048],
                 waits=[stage_free[1]])
        stage_free[0] = stage_free[1] = ("SA", ag)
        va = dve(lambda e: e.tensor_mul(WBC[:, 0:1024], WBC[:, 0:1024], WST[0][:]),
                 waits=[("SA", ag), ("SX", s2)])
        vb = dve(lambda e: e.tensor_mul(WBC[:, 1024:2048], WBC[:, 1024:2048], WST[1][:]))
        xr = dma("sync", FA[md][:], x_h[md * 128:(md + 1) * 128, :],
                 waits=[("SA", ag)])
        vc = dve(lambda e, md=md: e.tensor_add(FA[md][:], FA[md][:], WBC[:]),
                 waits=[("SX", xr), ("SV", vb)])
        out_done = dma("sync", out_h[md * 128:(md + 1) * 128, :], FA[md][:],
                       waits=[("SV", vc)])
        ag_a = ag
    E.add("sync", None, waits=[("SX", out_done)])

    _fin(E, nc, es)
    return nc


def _fin(E, nc, es):
    with ExitStack() as sem_es:
        sems = {s: sem_es.enter_context(nc.semaphore(s)) for s in SEMS}
        with nc.Block() as block:
            @block.sync
            def _(e):
                E.emit_stream("sync", e, sems)

            @block.tensor
            def _(e):
                E.emit_stream("tensor", e, sems)

            @block.vector
            def _(e):
                E.emit_stream("vector", e, sems)

            @block.scalar
            def _(e):
                E.emit_stream("scalar", e, sems)

            @block.gpsimd
            def _(e):
                E.emit_stream("gpsimd", e, sems)
    es.close()



def _prep_weights(inputs):
    f32 = np.float32
    W = {}
    win = np.asarray(inputs["L0_attn_in_proj_w"], f32)
    W["win"] = np.ascontiguousarray(win.reshape(8, 128, 32, 128).transpose(2, 0, 1, 3))
    W["wxp"] = np.ascontiguousarray(np.asarray(inputs["L0_attn_x_proj_w"], f32).reshape(16, 128, 96))
    W["wdt"] = np.ascontiguousarray(np.asarray(inputs["L0_attn_dt_proj_w"], f32).reshape(64, 16, 128).transpose(1, 0, 2))
    wout = np.asarray(inputs["L0_attn_out_proj_w"], f32)
    W["wout"] = np.ascontiguousarray(wout.reshape(16, 128, 8, 128).transpose(0, 2, 1, 3))
    wglu = np.asarray(inputs["L0_attn_glu_w"], f32)
    W["wglu"] = np.ascontiguousarray(wglu.reshape(8, 128, 16, 128).transpose(2, 0, 1, 3))
    w1 = np.asarray(inputs["L0_ffn_W1"], f32)
    W["w1"] = np.ascontiguousarray(w1.reshape(8, 8, 128, 4, 8, 128).transpose(0, 3, 4, 1, 2, 5))
    w2 = np.asarray(inputs["L0_ffn_W2"], f32)
    W["w2"] = np.ascontiguousarray(w2.reshape(8, 4, 8, 128, 8, 128).transpose(0, 1, 4, 2, 3, 5))
    W["wr"] = np.ascontiguousarray(np.asarray(inputs["L0_ffn_Wr"], f32).reshape(8, 128, 8))
    W["gcol"] = np.ascontiguousarray(np.asarray(inputs["L0_attn_ln_gamma"], f32).reshape(8, 128).T)
    W["bcol"] = np.ascontiguousarray(np.asarray(inputs["L0_attn_ln_beta"], f32).reshape(8, 128).T)
    W["cw"] = np.ascontiguousarray(np.asarray(inputs["L0_attn_conv_w"], f32).reshape(16, 128, 4))
    W["cb"] = np.ascontiguousarray(np.asarray(inputs["L0_attn_conv_b"], f32).reshape(16, 128).T)
    W["dtb"] = np.ascontiguousarray(np.asarray(inputs["L0_attn_dt_proj_b"], f32).reshape(16, 128).T)
    W["alog"] = np.ascontiguousarray(np.asarray(inputs["L0_attn_A_log"], f32).reshape(16, 128, 16))
    W["dcol"] = np.ascontiguousarray(np.asarray(inputs["L0_attn_D"], f32).reshape(16, 128).T)
    b1 = np.asarray(inputs["L0_ffn_b1"], f32)
    W["b1c"] = np.ascontiguousarray(b1.reshape(8, 32, 128).transpose(0, 2, 1))
    W["b2"] = np.ascontiguousarray(np.asarray(inputs["L0_ffn_b2"], f32))
    W["glub"] = np.ascontiguousarray(np.asarray(inputs["L0_attn_glu_b"], f32).reshape(16, 128).T)
    return W


def kernel(**inputs):
    W = _prep_weights(inputs)
    x = np.asarray(inputs["L0_attn_x"], np.float32)
    nc = build()
    in_maps = []
    for b in range(8):
        m = dict(W)
        m["x"] = np.ascontiguousarray(x[b].T)
        m["coremask"] = (np.arange(8) < b).astype(np.float32).reshape(8, 1)
        in_maps.append(m)
    res = run_bass_kernel_spmd(nc, in_maps, core_ids=list(range(8)))
    out = np.stack([res.results[b]["out"].T for b in range(8)], axis=0)
    return np.ascontiguousarray(out, dtype=np.float32)


# version 19
# speedup vs baseline: 1.0097x; 1.0097x over previous
import numpy as np
from contextlib import ExitStack
import concourse.bass as bass
import concourse.mybir as mybir
from concourse.bass_utils import run_bass_kernel_spmd

F32 = mybir.dt.float32
BF16 = mybir.dt.bfloat16
AF = mybir.ActivationFunctionType
ALU = mybir.AluOpType

D, T, DI, DS, DTR, NE = 1024, 2048, 2048, 16, 64, 8
NKC = D // 128
NCI = DI // 128
NTT = T // 512
CAP = 2560.0
EPS = 1e-5

ENGS = ("sync", "tensor", "vector", "scalar", "gpsimd")
SEMS = ("SX", "SG", "SM", "SA", "SV", "SP", "SC")


class Emitter:
    def __init__(self):
        self.sched = []
        self.cnt = {s: 0 for s in SEMS}

    def c(self, sem):
        return self.cnt[sem]

    def add(self, eng, fn, waits=(), inc=None, amt=1):
        waits = [(s, v) for (s, v) in waits if v > 0]
        for s, v in waits:
            assert v <= self.cnt[s], f"wait on future: {s} {v} > {self.cnt[s]}"
        if inc is not None:
            self.cnt[inc] += amt
        self.sched.append((eng, tuple(waits), fn, inc, amt))
        return self.cnt[inc] if inc is not None else None

    def emit_stream(self, eng_name, eng, sems):
        lastw = {}
        for item_eng, waits, fn, inc, amt in self.sched:
            if item_eng != eng_name:
                continue
            for s, v in waits:
                if lastw.get(s, 0) < v:
                    eng.wait_ge(sems[s], v)
                    lastw[s] = v
            inst = fn(eng) if fn is not None else None
            if inc is not None:
                inst.then_inc(sems[inc], amt)


def build(stop=99, dev_out=()):
    nc = bass.Bass("TRN2", target_bir_lowering=False, debug=False)

    def dt_(name, shape, dtype, kind="ExternalInput"):
        return nc.dram_tensor(name, shape, dtype, kind=kind)

    def scratch(name, shape, dtype):
        kind = "ExternalOutput" if name in dev_out else "Internal"
        return nc.dram_tensor(name, shape, dtype, kind=kind)

    x_h = dt_("x", [D, T], F32)
    win_h = dt_("win", [32, NKC, 128, 128], F32)
    wxp_h = dt_("wxp", [NCI, 128, 96], F32)
    wdt_h = dt_("wdt", [NCI, 64, 128], F32)
    wout_h = dt_("wout", [NCI, 8, 128, 128], F32)
    wglu_h = dt_("wglu", [16, NKC, 128, 128], F32)
    w1_h = dt_("w1", [NE, 4, 8, NKC, 128, 128], F32)
    w2_h = dt_("w2", [NE, 4, 8, 8, 128, 128], F32)
    wr_h = dt_("wr", [NKC, 128, NE], F32)
    gcol_h = dt_("gcol", [128, NKC], F32)
    bcol_h = dt_("bcol", [128, NKC], F32)
    cw_h = dt_("cw", [NCI, 128, 4], F32)
    cb_h = dt_("cb", [128, NCI], F32)
    dtb_h = dt_("dtb", [128, NCI], F32)
    alog_h = dt_("alog", [NCI, 128, DS], F32)
    dcol_h = dt_("dcol", [128, NCI], F32)
    b1c_h = dt_("b1c", [NE, 128, 32], F32)
    b2_h = dt_("b2", [NE, D], F32)
    glub_h = dt_("glub", [128, 16], F32)
    mask_h = dt_("coremask", [8, 1], F32)
    out_h = dt_("out", [D, T], F32, kind="ExternalOutput")

    u_dram = scratch("u_dram", [DI, T], BF16)
    z_dram = scratch("z_dram", [DI, T], BF16)
    yg_dram = scratch("yg_dram", [DI, T], BF16)
    yacc_dram = scratch("yacc_dram", [DI, T], F32)
    s_dram = scratch("s_dram", [D, T], F32)
    bc_dram = scratch("bc_dram", [32, T], BF16)
    stat_dram = scratch("stat_dram", [8, T], F32)
    rt_dram = scratch("rt_dram", [32, T], F32)
    w8_dram = scratch("w8_dram", [8, T], F32)
    xnbf_dram = scratch("xnbf_dram", [D, T], BF16)
    h_dram = scratch("h_dram", [NE * 4096, T], BF16)
    dt_dram = scratch("dt_dram", [DI, T], BF16)
    dtu_dram = scratch("dtu_dram", [DI, T], BF16)
    cc_in = scratch("cc_in", [16, 1], F32)
    cc_out = scratch("cc_out", [128, 1], F32)

    es = ExitStack()
    sb = lambda name, shape, dtype: es.enter_context(nc.sbuf_tensor(name, shape, dtype))
    ps = lambda name, shape, dtype: es.enter_context(nc.psum_tensor(name, shape, dtype))

    XNBF = [sb(f"xnbf{i}", [128, T], BF16) for i in range(8)]
    BA = [sb(f"ba{i}", [128, T], BF16) for i in range(8)]
    FA = [sb(f"fa{i}", [128, T], F32) for i in range(8)]
    WST = [sb(f"wst{i}", [128, 1024], F32) for i in range(2)]
    WB1 = [sb(f"wb1_{i}", [128, 1024], BF16) for i in range(4)]
    WB2 = [sb(f"wb2_{i}", [128, 1024], BF16) for i in range(4)]
    WBC = sb("wbc", [128, T], F32)
    XIP = [sb(f"xip{i}", [128, T + 4], BF16) for i in range(2)]
    UBUF = [sb(f"ubuf{i}", [128, T], BF16) for i in range(2)]
    FH2 = sb("fh2", [128, T], F32)
    xdbT = FH2
    bcbf = sb("bcbf", [32, T], BF16)
    dtraw = sb("dtraw", [64, T], BF16)
    HR = [sb(f"hr{i}", [128, 1024], BF16) for i in range(2)]
    ident = sb("ident", [128, 128], F32)
    ones = sb("ones", [128, 1], F32)
    Asb = [sb(f"asb{i}", [128, DS], F32) for i in range(NCI)]
    gcol = sb("gcols", [128, NKC], F32)
    bcol = sb("bcols", [128, NKC], F32)
    cwsb = sb("cwsb", [128, 4 * NCI], F32)
    cbsb = sb("cbsb", [128, NCI], F32)
    dtbsb = sb("dtbsb", [128, NCI], F32)
    dcsb = sb("dcsb", [128, NCI], F32)
    glubsb = sb("glubsb", [128, 16], F32)
    wrsb = sb("wrsb", [128, NKC * NE], F32)
    wxpbf = [sb(f"wxpbf{i}", [128, 96], BF16) for i in range(2)]
    wdtbf = [sb(f"wdtbf{i}", [64, 128], BF16) for i in range(2)]
    b1sb = [sb(f"b1sb{i}", [128, 32], F32) for i in range(2)]
    b2f = sb("b2f", [8, D], F32)
    b2b = sb("b2b", [8, D], BF16)
    Gsb = sb("gsb", [8, 16], F32)
    msk = sb("msk", [8, 1], F32)
    offs = sb("offs", [8, 2], F32)
    mx8 = sb("mx8", [128, 8], F32)
    eqs = sb("eqs", [128, 32], F32)
    tvs = sb("tvs", [128, 2], F32)
    TP = sb("tp", [128, 128], F32)
    LG = sb("lg", [128, 128], F32)

    PA = ps("pa", [128, T], F32)
    PB = ps("pb", [128, T], F32)

    E = Emitter()

    def dma(q, out, in_, waits=(), **kw):
        sem = "SX" if q == "sync" else "SG"
        return E.add(q, lambda e: e.dma_start(out=out, in_=in_, **kw),
                     waits=waits, inc=sem, amt=16)

    def mm(out, lhsT, rhs, start, stop, waits=(), inc=False):
        return E.add("tensor",
                     lambda e: e.matmul(out, lhsT, rhs, start=start, stop=stop),
                     waits=waits, inc=("SM" if inc else None))

    def act(out, in_, func, bias=0.0, scale=1.0, waits=(), inc=True):
        return E.add("scalar",
                     lambda e: e.activation(out, in_, func, bias=bias, scale=scale),
                     waits=waits, inc=("SA" if inc else None))

    def dve(fn, waits=(), inc=True):
        return E.add("vector", fn, waits=waits, inc=("SV" if inc else None))

    def pool(fn, waits=(), inc=True):
        return E.add("gpsimd", fn, waits=waits, inc=("SP" if inc else None))

    def bcast(q, dst, src_row, waits=()):
        return dma(q, dst, src_row.partition_broadcast(128).squeeze(1), waits=waits)

    stage_free = [("SA", 0), ("SA", 0)]
    stage_g = [0]

    def stage_load(dram_apm, nrows, ncols, q="sync", extra=()):
        st = stage_g[0] % 2
        stage_g[0] += 1
        a = dram_apm.shape[0]
        d = dma(q, WST[st][0:nrows, 0:a * ncols].rearrange("p (a b) -> p a b", a=a),
                dram_apm.rearrange("a p m -> p a m"),
                waits=[stage_free[st]] + list(extra))
        return st, d

    class Ring:
        def __init__(self, ring):
            self.ring = ring
            self.slot_free = [0] * len(ring)

        def load(self, dram_apm, q="sync", extra=(), cast_eng=None):
            gi = getattr(self, "g", 0)
            self.g = gi + 1
            sl = gi % len(self.ring)
            st, d = stage_load(dram_apm, 128, 128, q=q,
                               extra=list(extra) + [("SM", self.slot_free[sl])])
            dw = [("SX" if q == "sync" else "SG", d)]
            if cast_eng == "gpsimd":
                a = pool(lambda e, sl=sl, st=st: e.tensor_copy(self.ring[sl][:], WST[st][:]),
                         waits=dw)
                stage_free[st] = ("SP", a)
            else:
                a = act(self.ring[sl][:], WST[st][:], AF.Copy, waits=dw)
                stage_free[st] = ("SA", a)
            return sl, a

        def consumed(self, sl):
            self.slot_free[sl] = E.c("SM")

    ring1 = Ring(WB1)
    ring2 = Ring(WB2)

    pool(lambda e: e.memset(ident[:], 0.0), inc=False)
    ident_v = pool(lambda e: e.affine_select(out=ident[:], in_=ident[:],
                                             compare_op=ALU.not_equal, fill=1.0, base=0,
                                             pattern=[[-1, 128]], channel_multiplier=1))
    ones_v = dve(lambda e: e.memset(ones[:], 1.0))
    dma("gpsimd", gcol[:], gcol_h[:])
    dma("gpsimd", bcol[:], bcol_h[:])
    for c in range(NCI):
        dma("gpsimd", cwsb[:, c * 4:(c + 1) * 4], cw_h[c])
    dma("gpsimd", cbsb[:], cb_h[:])
    dma("gpsimd", dtbsb[:], dtb_h[:])
    dma("gpsimd", dcsb[:], dcol_h[:])
    dma("gpsimd", glubsb[:], glub_h[:])
    dma("gpsimd", msk[:], mask_h[:])
    b2d = dma("gpsimd", b2f[:], b2_h[:])
    for kc in range(NKC):
        dma("gpsimd", wrsb[:, kc * 8:(kc + 1) * 8], wr_h[kc])
    for c in range(NCI):
        v = dma("gpsimd", Asb[c][:], alog_h[c])
        act(Asb[c][:], Asb[c][:], AF.Exp, waits=[("SG", v)])
        dve(lambda e, t=Asb[c]: e.tensor_scalar_mul(t[:], t[:], -1.0),
            waits=[("SA", E.c("SA"))])
    b2cast = dve(lambda e: e.tensor_copy(b2b[:], b2f[:]), waits=[("SG", b2d)])
    consts_g = E.c("SG")

    xld = [0] * NKC
    xld[0] = dma("sync", FA[0][:], x_h[0:128, :])
    xld[1] = dma("sync", FA[1][:], x_h[128:256, :])
    sumx_mm = [0] * NKC
    sqm_mm = [0] * NKC
    sq_a = [0] * NKC
    for kc in range(NKC):
        xb = FA[kc % 2]
        for tt in range(NTT):
            mm(PB[0:1, tt * 512:(tt + 1) * 512], ones[:], xb[:, tt * 512:(tt + 1) * 512],
               start=(kc == 0), stop=(kc == NKC - 1),
               waits=[("SX", xld[kc]), ("SV", ones_v)], inc=(tt == NTT - 1))
        sumx_mm[kc] = E.c("SM")
        sq = act(FA[2][:], xb[:], AF.Square,
                 waits=[("SX", xld[kc])] + ([("SM", sqm_mm[kc - 1])] if kc >= 1 else []))
        sq_a[kc] = sq
        for tt in range(NTT):
            mm(PA[0:1, tt * 512:(tt + 1) * 512], ones[:], FA[2][:, tt * 512:(tt + 1) * 512],
               start=(kc == 0), stop=(kc == NKC - 1),
               waits=[("SA", sq)], inc=(tt == NTT - 1))
        sqm_mm[kc] = E.c("SM")
        if kc + 2 < NKC:
            xld[kc + 2] = dma("sync", xb[:], x_h[(kc + 2) * 128:(kc + 3) * 128, :],
                              waits=[("SA", sq), ("SM", sumx_mm[kc])])
    mm_done = E.c("SM")
    mu1 = FA[3][0:1, :]
    rstd1 = FA[4][0:1, :]
    vtmp = FA[5][0:1, :]
    act(mu1, PB[0:1, :], AF.Copy, scale=1.0 / D, waits=[("SM", mm_done)])
    act(vtmp, PA[0:1, :], AF.Copy, scale=1.0 / D)
    a_musq = act(FA[2][0:1, :], mu1, AF.Square)
    dve(lambda e: e.tensor_sub(vtmp, vtmp, FA[2][0:1, :]), waits=[("SA", a_musq)])
    v_var = dve(lambda e: e.tensor_scalar_add(vtmp, vtmp, EPS))
    act(vtmp, vtmp, AF.Ln, waits=[("SV", v_var)])
    a_r1 = act(rstd1, vtmp, AF.Exp, scale=-0.5)
    d_st1 = dma("gpsimd", stat_dram[0:1, :], mu1, waits=[("SA", E.c("SA")), ("SG", consts_g)])
    d_st2 = dma("gpsimd", stat_dram[1:2, :], rstd1, waits=[("SA", a_r1)])
    mu1b, rstd1b = FA[3], FA[4]
    d_mu1b = bcast("gpsimd", mu1b[:], stat_dram[0:1, :], waits=[("SG", d_st1)])
    d_rs1b = bcast("gpsimd", rstd1b[:], stat_dram[1:2, :], waits=[("SG", d_st2)])

    if stop < 2:
        _fin(E, nc, es)
        return nc

    xld = [0] * NKC
    xld[0] = dma("sync", FA[0][:], x_h[0:128, :],
                 waits=[("SA", sq_a[7]), ("SM", mm_done)])
    xld[1] = dma("sync", FA[1][:], x_h[128:256, :])
    sxn = xdbT[0:1, :]
    sxq = xdbT[32:33, :]
    sq2_a = [0] * NKC
    v_sx = v_sq = v_lg = 0
    xn_mm = [0] * NKC
    for kc in range(NKC):
        xb = FA[kc % 2]
        xn = FA[5]
        w_ = [("SX", xld[kc]), ("SG", max(d_mu1b, d_rs1b))]
        if kc >= 1:
            w_ += [("SM", xn_mm[kc - 1]), ("SA", sq2_a[kc - 1])]
        v1 = dve(lambda e, xb=xb: e.tensor_sub(xn[:], xb[:], mu1b[:]), waits=w_)
        v2 = dve(lambda e: e.tensor_mul(xn[:], xn[:], rstd1b[:]))
        act(XNBF[kc][:], xn[:], AF.Copy, waits=[("SV", v2)])
        for tt in range(NTT):
            mm(PB[0:1, tt * 512:(tt + 1) * 512], ones[:], xn[:, tt * 512:(tt + 1) * 512],
               start=True, stop=True, waits=[("SV", v2), ("SV", v_sx)],
               inc=(tt == NTT - 1))
        mmx = E.c("SM")
        if kc == 0:
            v_sx = dve(lambda e: e.tensor_copy(sxn, PB[0:1, :]), waits=[("SM", mmx)])
        else:
            v_sx = dve(lambda e: e.tensor_add(sxn, sxn, PB[0:1, :]), waits=[("SM", mmx)])
        sq2 = act(FA[2][:], xn[:], AF.Square,
                  waits=[("SV", v2)] + ([("SM", xn_mm[kc - 1])] if kc >= 1 else [("SM", mm_done)]))
        sq2_a[kc] = sq2
        for tt in range(NTT):
            mm(PA[0:1, tt * 512:(tt + 1) * 512], ones[:], FA[2][:, tt * 512:(tt + 1) * 512],
               start=True, stop=True,
               waits=[("SA", sq2), ("SV", v_sq), ("SV", v_lg)],
               inc=(tt == NTT - 1))
        mmq = E.c("SM")
        if kc == 0:
            v_sq = dve(lambda e: e.tensor_copy(sxq, PA[0:1, :]), waits=[("SM", mmq)])
        else:
            v_sq = dve(lambda e: e.tensor_add(sxq, sxq, PA[0:1, :]), waits=[("SM", mmq)])
        for ts_ in range(16):
            mm(PA[:, 512 + ts_ * 8:512 + (ts_ + 1) * 8],
               xn[:, ts_ * 128:(ts_ + 1) * 128],
               wrsb[:, kc * 8:(kc + 1) * 8],
               start=True, stop=True,
               waits=[("SG", consts_g), ("SV", v_sq)],
               inc=(ts_ == 15))
        mmr = E.c("SM")
        if kc == 0:
            v_lg = dve(lambda e: e.tensor_copy(LG[:], PA[:, 512:640]), waits=[("SM", mmr)])
        else:
            v_lg = dve(lambda e: e.tensor_add(LG[:], LG[:], PA[:, 512:640]), waits=[("SM", mmr)])
        xn_mm[kc] = E.c("SM")
        if kc + 2 < NKC:
            xld[kc + 2] = dma("sync", xb[:], x_h[(kc + 2) * 128:(kc + 3) * 128, :],
                              waits=[("SA", sq2), ("SM", xn_mm[kc])])
    rt_mm = E.c("SM")
    mu2 = FA[2][0:1, :]
    rstd2 = FA[5][0:1, :]
    act(mu2, sxn, AF.Copy, scale=1.0 / D, waits=[("SV", v_lg)])
    act(rstd2, sxq, AF.Copy, scale=1.0 / D)
    a_m2 = act(FA[0][0:1, :], mu2, AF.Square)
    dve(lambda e: e.tensor_sub(rstd2, rstd2, FA[0][0:1, :]), waits=[("SA", a_m2)])
    v_v2 = dve(lambda e: e.tensor_scalar_add(rstd2, rstd2, EPS))
    act(rstd2, rstd2, AF.Ln, waits=[("SV", v_v2)])
    a_r2 = act(rstd2, rstd2, AF.Exp, scale=-0.5)
    d_s3 = dma("gpsimd", stat_dram[2:3, :], mu2, waits=[("SA", a_r2)])
    d_s4 = dma("gpsimd", stat_dram[3:4, :], rstd2, waits=[("SA", a_r2)])

    tp_guard = [0, 0]
    last_tr = 0
    ev_q = 0
    for tk in range(16):
        lg = LG[:, tk * 8:(tk + 1) * 8]
        vmax = dve(lambda e, lg=lg: e.max(out=mx8[:], in_=lg), waits=[("SV", v_lg)])
        dve(lambda e, lg=lg: e.tensor_scalar(eqs[:, 0:8], lg, mx8[:, 0:1], None, ALU.is_equal),
            waits=[("SV", vmax)])
        dve(lambda e, lg=lg: e.tensor_scalar(eqs[:, 8:16], lg, mx8[:, 1:2], None, ALU.is_equal))
        vt = dve(lambda e: e.tensor_sub(tvs[:, 0:1], mx8[:, 0:1], mx8[:, 1:2]))
        a_t1 = act(tvs[:, 0:1], tvs[:, 0:1], AF.Sigmoid, waits=[("SV", vt)])
        a_t2 = act(tvs[:, 1:2], tvs[:, 0:1], AF.Copy, scale=-1.0, bias=1.0)
        dve(lambda e: e.tensor_scalar_mul(eqs[:, 16:24], eqs[:, 0:8], tvs[:, 0:1]),
            waits=[("SA", a_t2)])
        dve(lambda e: e.tensor_scalar_mul(eqs[:, 24:32], eqs[:, 8:16], tvs[:, 1:2]))
        dve(lambda e: e.tensor_copy(TP[:, 0:8], eqs[:, 0:8]),
            waits=[("SM", last_tr)])
        dve(lambda e: e.tensor_copy(TP[:, 32:40], eqs[:, 8:16]))
        dve(lambda e: e.tensor_copy(TP[:, 64:72], eqs[:, 16:24]))
        p4 = dve(lambda e: e.tensor_copy(TP[:, 96:104], eqs[:, 24:32]))
        tpp = PB[:, 256 + (tk % 2) * 512:384 + (tk % 2) * 512]
        last_tr = E.add("tensor", lambda e, tpp=tpp: e.transpose(tpp, TP[:], ident[:]),
                        waits=[("SV", p4), ("SA", E.c("SA")), ("SV", tp_guard[tk % 2]),
                               ("SP", ident_v)],
                        inc="SM")
        dve(lambda e, tpp=tpp, tk=tk: e.tensor_copy(FA[6][0:8, tk * 128:(tk + 1) * 128], tpp[0:8, :]),
            waits=[("SM", last_tr)])
        dve(lambda e, tpp=tpp, tk=tk: e.tensor_copy(FA[6][32:40, tk * 128:(tk + 1) * 128], tpp[32:40, :]))
        dve(lambda e, tpp=tpp, tk=tk: e.tensor_copy(FA[6][64:72, tk * 128:(tk + 1) * 128], tpp[64:72, :]))
        ev_q = dve(lambda e, tpp=tpp, tk=tk: e.tensor_copy(FA[6][96:104, tk * 128:(tk + 1) * 128], tpp[96:104, :]))
        tp_guard[tk % 2] = ev_q
    r1 = dve(lambda e: e.tensor_tensor_scan(FA[7][0:8, :], FA[6][0:8, :], FA[6][0:8, :],
                                            0.0, ALU.add, ALU.bypass), waits=[("SV", ev_q)])
    r2 = dve(lambda e: e.tensor_tensor_scan(FA[7][32:40, :], FA[6][32:40, :], FA[6][32:40, :],
                                            0.0, ALU.add, ALU.bypass))
    d_t1 = dma("gpsimd", cc_in[0:8, :], FA[7][0:8, T - 1:T], waits=[("SV", r1)])
    d_t2 = dma("gpsimd", cc_in[8:16, :], FA[7][32:40, T - 1:T], waits=[("SV", r2)])
    E.add("gpsimd",
          lambda e: e.collective_compute(
              "AllGather", ALU.bypass, replica_groups=[list(range(8))],
              ins=[cc_in.ap().opt()], outs=[cc_out.ap().opt()]),
          waits=[("SG", max(d_t1, d_t2))], inc="SC")
    dma("sync", rt_dram[0:8, :], FA[6][64:72, :], waits=[("SV", r2)])
    dma("sync", rt_dram[8:16, :], FA[6][96:104, :])
    dma("sync", rt_dram[16:24, :], FA[7][0:8, :])
    dma("sync", rt_dram[24:32, :], FA[7][32:40, :])
    rt_parked = E.c("SX")

    if stop < 4:
        _fin(E, nc, es)
        return nc

    mu2b, rstd2b = FA[3], FA[4]
    d_m2b = bcast("gpsimd", mu2b[:], stat_dram[2:3, :], waits=[("SG", d_s3)])
    d_r2b = bcast("gpsimd", rstd2b[:], stat_dram[3:4, :], waits=[("SG", d_s4)])
    hln_a = 0
    for kc in range(NKC):
        t = FA[5]
        v1 = dve(lambda e, kc=kc: e.tensor_sub(t[:], XNBF[kc][:], mu2b[:]),
                 waits=[("SG", max(d_m2b, d_r2b)), ("SX", rt_parked)])
        dve(lambda e: e.tensor_mul(t[:], t[:], rstd2b[:]))
        hln_a = dve(lambda e, kc=kc: e.tensor_scalar(
            BA[kc][:], t[:], gcol[:, kc:kc + 1], bcol[:, kc:kc + 1],
            ALU.mult, ALU.add))

    if stop < 5:
        _fin(E, nc, es)
        return nc

    pgA = [0, 0]
    ordered_mc = list(range(16, 32)) + list(range(16))
    conv_v = 0
    uc_parked = [0] * NCI
    zc_parked = [0] * NCI
    memset_done = {}
    for idx, mc in enumerate(ordered_mc):
        sl, aw = ring1.load(win_h[mc])
        P_ = PA if idx % 2 == 0 else PB
        for kc in range(NKC):
            for tt in range(NTT):
                mm(P_[:, tt * 512:(tt + 1) * 512],
                   WB1[sl][:, kc * 128:(kc + 1) * 128],
                   BA[kc][:, tt * 512:(tt + 1) * 512],
                   start=(kc == 0), stop=(kc == NKC - 1),
                   waits=[("SA", aw), ("SV", hln_a), ("SA", pgA[idx % 2])],
                   inc=(kc == NKC - 1 and tt == NTT - 1))
        gmm = E.c("SM")
        ring1.consumed(sl)
        if mc >= 16:
            c = mc - 16
            zb = XIP[idx % 2]
            ev = act(zb[:, 0:T], P_[:], AF.Silu,
                     waits=[("SM", gmm), ("SX", zc_parked[c - 2] if c >= 2 else 0)])
            zc_parked[c] = dma("sync", z_dram[c * 128:(c + 1) * 128, :], zb[:, 0:T],
                               waits=[("SA", ev)])
            pgA[idx % 2] = ev
        else:
            c = mc
            xp = XIP[idx % 2]
            if c in (0, 1):
                memset_done[c] = dve(lambda e, xp=xp: e.memset(xp[:, 0:4], 0.0),
                                     waits=[("SX", zc_parked[14 + c])])
            ev = act(xp[:, 4:4 + T], P_[:], AF.Copy,
                     waits=[("SM", gmm), ("SV", conv_v),
                            ("SV", memset_done.get(c, 0)),
                            ("SX", zc_parked[14 + c] if c in (0, 1) else 0)])
            pgA[idx % 2] = ev
            t0, t1 = FA[0], FA[1]
            dve(lambda e, c=c, xp=xp: e.tensor_scalar_mul(
                t0[:], xp[:, 1:1 + T], cwsb[:, c * 4:c * 4 + 1]),
                waits=[("SA", ev), ("SG", consts_g)])
            dve(lambda e, c=c, xp=xp: e.scalar_tensor_tensor(
                t1[:], xp[:, 2:2 + T], cwsb[:, c * 4 + 1:c * 4 + 2], t0[:], ALU.mult, ALU.add))
            dve(lambda e, c=c, xp=xp: e.scalar_tensor_tensor(
                t0[:], xp[:, 3:3 + T], cwsb[:, c * 4 + 2:c * 4 + 3], t1[:], ALU.mult, ALU.add))
            conv_v = dve(lambda e, c=c, xp=xp: e.scalar_tensor_tensor(
                t1[:], xp[:, 4:4 + T], cwsb[:, c * 4 + 3:c * 4 + 4], t0[:], ALU.mult, ALU.add))
            uev = act(UBUF[c % 2][:], t1[:], AF.Silu, bias=cbsb[:, c:c + 1],
                      waits=[("SV", conv_v), ("SX", uc_parked[c - 2] if c >= 2 else 0)])
            uc_parked[c] = dma("sync", u_dram[c * 128:(c + 1) * 128, :], UBUF[c % 2][:],
                               waits=[("SA", uev)])

    if stop < 6:
        _fin(E, nc, es)
        return nc

    uld = [0] * NCI
    uld[0] = dma("sync", UBUF[0][:], u_dram[0:128, :], waits=[("SX", uc_parked[15])])
    uld[1] = dma("sync", UBUF[1][:], u_dram[128:256, :])
    wx_cast = [0, 0]
    xp_mm_c = [0] * NCI
    for c in range(NCI):
        wsl = c % 2
        st, dwx = stage_load(wxp_h[c:c + 1], 128, 96, q="gpsimd")
        aw = act(wxpbf[wsl][:], WST[st][0:128, 0:96], AF.Copy,
                 waits=[("SG", dwx)] + ([("SM", xp_mm_c[c - 2])] if c >= 2 else []))
        stage_free[st] = ("SA", aw)
        wx_cast[wsl] = aw
        for tt in range(NTT):
            mm(PA[0:96, tt * 512:(tt + 1) * 512], wxpbf[wsl][:],
               UBUF[c % 2][:, tt * 512:(tt + 1) * 512],
               start=(c == 0), stop=(c == NCI - 1),
               waits=[("SA", aw), ("SX", uld[c]), ("SA", pgA[0]), ("SA", pgA[1])],
               inc=(tt == NTT - 1))
        xp_mm_c[c] = E.c("SM")
        if c + 2 < NCI:
            uld[c + 2] = dma("sync", UBUF[c % 2][:], u_dram[(c + 2) * 128:(c + 3) * 128, :],
                             waits=[("SM", xp_mm_c[c])])
    xp_mm = E.c("SM")
    ev1 = act(xdbT[0:96, :], PA[0:96, :], AF.Copy, waits=[("SM", xp_mm)])
    ev2 = act(dtraw[:], xdbT[0:64, :], AF.Copy)
    ev3 = act(bcbf[:], xdbT[64:96, :], AF.Copy)
    d_b = dma("sync", bc_dram[:, :], bcbf[:], waits=[("SA", ev3)])
    bc_parked = d_b

    if stop < 7:
        _fin(E, nc, es)
        return nc

    l1 = dma("gpsimd", FA[0][0:8, :], rt_dram[0:8, :],
             waits=[("SV", E.c("SV")), ("SA", E.c("SA"))])
    dma("gpsimd", FA[0][32:40, :], rt_dram[8:16, :])
    dma("gpsimd", FA[0][64:72, :], rt_dram[16:24, :])
    l4 = dma("gpsimd", FA[0][96:104, :], rt_dram[24:32, :])
    lg_ = dma("gpsimd", Gsb[:], cc_out.ap().rearrange("(r k) o -> r (k o)", r=8),
              waits=[("SC", 1)])
    mm(PB[0:8, 0:1], Gsb[:, 0:8], msk[:], True, True,
       waits=[("SG", lg_), ("SA", ev3), ("SM", xp_mm)])
    mm(PB[0:8, 512:513], Gsb[:, 8:16], msk[:], True, True, inc=True)
    off_mm = E.c("SM")
    dve(lambda e: e.tensor_copy(offs[:, 0:1], PB[0:8, 0:1]), waits=[("SM", off_mm)])
    vo2 = dve(lambda e: e.tensor_copy(offs[:, 1:2], PB[0:8, 512:513]))
    dve(lambda e: e.tensor_scalar(FA[1][0:8, :], FA[0][64:72, :], offs[:, 0:1], None, ALU.add),
        waits=[("SV", vo2), ("SG", l4)])
    dve(lambda e: e.tensor_scalar(FA[1][32:40, :], FA[0][96:104, :], offs[:, 1:2], None, ALU.add))
    dve(lambda e: e.tensor_scalar(FA[1][0:8, :], FA[1][0:8, :], CAP + 0.5, None, ALU.is_le))
    dve(lambda e: e.tensor_scalar(FA[1][32:40, :], FA[1][32:40, :], CAP + 0.5, None, ALU.is_le))
    dve(lambda e: e.tensor_mul(FA[1][0:8, :], FA[1][0:8, :], FA[0][0:8, :]))
    dve(lambda e: e.tensor_mul(FA[1][32:40, :], FA[1][32:40, :], FA[0][32:40, :]))
    dve(lambda e: e.tensor_copy(FA[2][0:8, :], FA[1][32:40, :]))
    w8v = dve(lambda e: e.tensor_add(FA[1][0:8, :], FA[1][0:8, :], FA[2][0:8, :]))
    d_w8 = dma("sync", w8_dram[:, :], FA[1][0:8, :], waits=[("SV", w8v)])
    w8bf_v = bcbf[0:8, :]
    w8c = dve(lambda e: e.tensor_copy(w8bf_v, FA[1][0:8, :]), waits=[("SX", d_b)])

    f_dt, f_dtu, f_a0, f_a1 = FA[0], FA[1], FA[2], FA[3]
    f_bb = [FA[4], WBC]
    f_hh = [FA[5], FH2]
    f_yacc = [FA[6], FA[7]]
    uld2 = [0] * NCI
    wdt_cast = [0, 0]
    dt_mm_guard = [0, 0]
    spl_done = 0
    dtp_out = [0, 0]
    dtc_a = [0, 0]
    dtc_v = [0, 0]
    uld2[0] = dma("sync", UBUF[0][:], u_dram[0:128, :], waits=[("SX", uc_parked[15])])
    uld2[1] = dma("sync", UBUF[1][:], u_dram[128:256, :])
    for c in range(NCI):
        wsl = c % 2
        st, dwt = stage_load(wdt_h[c:c + 1], 64, 128, q="gpsimd",
                             extra=[("SA", wdt_cast[wsl])])
        awt = act(wdtbf[wsl][:], WST[st][0:64, 0:128], AF.Copy,
                  waits=[("SG", dwt), ("SM", dt_mm_guard[wsl])])
        stage_free[st] = ("SA", awt)
        wdt_cast[wsl] = awt
        for tt in range(NTT):
            mm(PB[:, tt * 512:(tt + 1) * 512], wdtbf[wsl][:],
               dtraw[:, tt * 512:(tt + 1) * 512], start=True, stop=True,
               waits=[("SA", awt), ("SA", ev2), ("SA", spl_done),
                      ("SM", xp_mm), ("SV", w8c)],
               inc=(tt == NTT - 1))
        dt_mm = E.c("SM")
        dt_mm_guard[wsl] = dt_mm
        sp_w = [("SM", dt_mm), ("SV", dtc_v[wsl]), ("SA", dtc_a[wsl])]
        act(FA[4][:], PB[:], AF.Relu, bias=dtbsb[:, c:c + 1], waits=sp_w)
        act(f_dtu[:], PB[:], AF.Abs, bias=dtbsb[:, c:c + 1])
        act(f_dtu[:], f_dtu[:], AF.Exp, scale=-1.0)
        spl = act(f_dtu[:], f_dtu[:], AF.Ln, bias=1.0)
        spl_done = spl
        vdt = dve(lambda e: e.tensor_add(f_dt[:], FA[4][:], f_dtu[:]),
                  waits=[("SA", spl)])
        vdtu = dve(lambda e, c=c: e.tensor_mul(f_dtu[:], f_dt[:], UBUF[c % 2][:]),
                   waits=[("SX", uld2[c])])
        if c + 2 < NCI:
            uld2[c + 2] = dma("sync", UBUF[c % 2][:], u_dram[(c + 2) * 128:(c + 3) * 128, :],
                              waits=[("SV", vdtu)])
        ca = act(BA[wsl][:], f_dt[:], AF.Copy,
                 waits=[("SV", vdt), ("SX", dtp_out[wsl])])
        cv = dve(lambda e, wsl=wsl: e.tensor_copy(BA[2 + wsl][:], f_dtu[:]),
                 waits=[("SX", dtp_out[wsl])])
        dtc_a[wsl] = ca
        dtc_v[wsl] = cv
        dma("sync", dt_dram[c * 128:(c + 1) * 128, :], BA[wsl][:], waits=[("SA", ca)])
        dtp_out[wsl] = dma("sync", dtu_dram[c * 128:(c + 1) * 128, :], BA[2 + wsl][:],
                           waits=[("SV", cv)])
    dtp_done = E.c("SX")

    dtbit = f_dt[:].bitcast(BF16)
    dtubit = f_dtu[:].bitcast(BF16)
    bbit = FA[4][:].bitcast(BF16)
    wbcbit = WBC[:].bitcast(BF16)

    HP = [PA[:, 0:1024], PA[:, 1024:2048], PB[:, 0:1024], PB[:, 1024:2048]]
    w1s = {"g": 0, "cur": None}
    hev_at = {}
    w1_gmm = {}
    hr_dma = [0, 0]
    b1_ld = [0, 0]
    G1 = NE * 4 * 8 * 2

    def w1_evac(g):
        e_, fs, mc, th = g // 64, (g // 16) % 4, (g // 2) % 8, g % 2
        pp = HP[g % 4]
        hv = act(HR[g % 2][:], pp, AF.Gelu,
                 bias=b1sb[e_ % 2][:, fs * 8 + mc:fs * 8 + mc + 1],
                 waits=[("SM", w1_gmm[g]), ("SG", b1_ld[e_ % 2]), ("SX", hr_dma[g % 2])])
        hev_at[g] = hv
        row = (e_ * 32 + fs * 8 + mc) * 128
        hr_dma[g % 2] = dma("sync", h_dram[row:row + 128, th * 1024:(th + 1) * 1024],
                            HR[g % 2][:], waits=[("SA", hv)])

    def w1_feed(kq):
        for _ in range(kq):
            g = w1s["g"]
            if g >= G1:
                return
            w1s["g"] = g + 1
            e_, fs, mc, th = g // 64, (g // 16) % 4, (g // 2) % 8, g % 2
            if th == 0:
                if mc == 0 and fs == 0:
                    b1_ld[e_ % 2] = dma("gpsimd", b1sb[e_ % 2][:], b1c_h[e_],
                                        waits=[("SA", hev_at.get(g - 3, 0))])
                w1s["cur"] = ring1.load(w1_h[e_, fs, mc], cast_eng="gpsimd")
            sl, aw = w1s["cur"]
            pp = HP[g % 4]
            for kc in range(NKC):
                for t2 in range(2):
                    mm(pp[:, t2 * 512:(t2 + 1) * 512],
                       WB1[sl][:, kc * 128:(kc + 1) * 128],
                       XNBF[kc][:, th * 1024 + t2 * 512:th * 1024 + (t2 + 1) * 512],
                       start=(kc == 0), stop=(kc == NKC - 1),
                       waits=[("SP", aw), ("SA", hev_at.get(g - 4, 0)),
                              ("SG", b1_ld[e_ % 2]), ("SA", spl_done)],
                       inc=(kc == NKC - 1 and t2 == 1))
            w1_gmm[g] = E.c("SM")
            if th == 1:
                ring1.consumed(sl)
            if g >= 2:
                w1_evac(g - 2)
            if g == G1 - 1:
                w1_evac(g - 1)
                w1_evac(g)

    acc_cons = [[0, 0], [0, 0]]
    dtl = [0, 0]
    dtul = [0, 0]
    zld = [0] * NCI
    ygout = [0] * NCI
    last_scan = 0
    scan_m2 = 0
    scan_at = {}
    pend2 = []
    last_dv = 0
    a_cnt = [0, 0]
    bb_ld = [0] * 4
    cc_ld = [0] * 4

    def flush_one():
        nonlocal last_dv
        (pit, pc, pq, ph, pnn, yw0) = pend2.pop(0)
        dve(lambda e, ph=ph, pnn=pnn: e.tensor_mul(ph[:], ph[:], BA[4 + pnn][:]),
            waits=[("SG", cc_ld[pnn])])
        if pq == 0 and pnn == 0:
            last_dv = dve(lambda e, pc=pc, ph=ph: e.tensor_copy(
                f_yacc[pc % 2][:], ph[:]), waits=yw0)
        else:
            last_dv = dve(lambda e, pc=pc, ph=ph: e.tensor_add(
                f_yacc[pc % 2][:], f_yacc[pc % 2][:], ph[:]), waits=yw0)

    yacc_out = [[0] * NCI for _ in range(4)]
    yacc_pref = [0] * NCI
    for q in range(4):
        for nn in range(4):
            n = q * 4 + nn
            bb_ld[nn] = bcast("gpsimd", BA[nn][:], bc_dram[n:n + 1, :],
                              waits=[("SX", bc_parked), ("SV", last_dv),
                                     ("SA", dtc_a[nn % 2] if q == 0 else 0),
                                     ("SX", dtp_done if q == 0 else 0)])
            cc_ld[nn] = bcast("gpsimd", BA[4 + nn][:], bc_dram[16 + n:17 + n, :],
                              waits=[("SV", last_dv),
                                     ("SV", dtc_v[nn % 2] if q == 0 else 0),
                                     ("SX", dtp_done if q == 0 else 0)])
        for c in range(NCI):
            sl_ = c % 2
            if q == 0 and c == 0:
                for pc_ in (0, 1):
                    dtl[pc_] = dma("sync", dtbit[:, pc_ * 2048:(pc_ + 1) * 2048],
                                   dt_dram[pc_ * 128:(pc_ + 1) * 128, :],
                                   waits=[("SX", dtp_done)])
                    dtul[pc_] = dma("sync", dtubit[:, pc_ * 2048:(pc_ + 1) * 2048],
                                    dtu_dram[pc_ * 128:(pc_ + 1) * 128, :],
                                    waits=[("SX", dtp_done)])
            nc_, nq_ = (c + 1, q) if c + 1 < NCI else (0, q + 1)
            if nq_ < 4 and not (q == 0 and c == 0):
                nsl = nc_ % 2
                dtl[nsl] = dma("sync", dtbit[:, nsl * 2048:(nsl + 1) * 2048],
                               dt_dram[nc_ * 128:(nc_ + 1) * 128, :],
                               waits=[("SA", acc_cons[0][nsl])])
                dtul[nsl] = dma("sync", dtubit[:, nsl * 2048:(nsl + 1) * 2048],
                                dtu_dram[nc_ * 128:(nc_ + 1) * 128, :],
                                waits=[("SV", acc_cons[1][nsl])])
            if q == 3:
                uld2[c] = dma("sync", UBUF[c % 2][:], u_dram[c * 128:(c + 1) * 128, :],
                              waits=[("SX", ygout[c - 2] if c >= 2 else 0),
                                     ("SV", last_dv if c >= 2 else 0)])
                zld[c] = dma("sync", XIP[c % 2][:, 0:T], z_dram[c * 128:(c + 1) * 128, :],
                             waits=[("SV", last_dv)])
            if q > 0:
                yacc_in = yacc_pref[c]
            else:
                yacc_in = 0
            if nq_ > 0 and nq_ < 4:
                sg2_ = yacc_out[nq_][nc_ - 2] if nc_ >= 2 else yacc_out[nq_ - 1][NCI - 2 + nc_]
                yacc_pref[nc_] = dma("sync", f_yacc[nc_ % 2][:],
                                     yacc_dram[nc_ * 128:(nc_ + 1) * 128, :],
                                     waits=[("SX", yacc_out[nq_ - 1][nc_]), ("SX", sg2_),
                                            ("SV", last_dv)])
            BS = [bbit[:, 0:2048], bbit[:, 2048:4096],
                  wbcbit[:, 0:2048], wbcbit[:, 2048:4096]]
            for nn in range(4):
                bv = dve(lambda e, nn=nn, sl_=sl_: e.tensor_mul(
                    BS[nn], dtubit[:, sl_ * 2048:(sl_ + 1) * 2048], BA[nn][:]),
                    waits=[("SG", bb_ld[nn]), ("SX", dtul[sl_])])
                acc_cons[1][sl_] = bv
            for nn in range(4):
                n = q * 4 + nn
                it = q * NCI * 4 + c * 4 + nn
                ab = [f_a0, f_a1][it % 2]
                fb = BS[nn]
                fh = f_hh[it % 2]
                if len(pend2) == 2:
                    flush_one()
                aa = act(ab[:], dtbit[:, sl_ * 2048:(sl_ + 1) * 2048],
                         AF.Exp, scale=Asb[c][:, n:n + 1],
                         waits=[("SX", dtl[sl_]), ("SV", scan_m2)])
                a_cnt[it % 2] = aa
                acc_cons[0][sl_] = aa
                sc = dve(lambda e, ab=ab, fb=fb, fh=fh: e.tensor_tensor_scan(
                    fh[:], ab[:], fb, 0.0, ALU.mult, ALU.add),
                    waits=[("SA", aa)])
                scan_at[it] = sc
                scan_m2 = last_scan
                last_scan = sc
                yw0 = []
                if nn == 0:
                    if q == 0:
                        yw0 = [("SX", yacc_out[0][c - 2] if c >= 2 else 0)]
                    else:
                        yw0 = [("SX", yacc_in)]
                pend2.append((it, c, q, fh, nn, yw0))
                w1_feed(2)
            while pend2:
                flush_one()
            if q < 3:
                yacc_out[q][c] = dma("sync", yacc_dram[c * 128:(c + 1) * 128, :],
                                     f_yacc[c % 2][:], waits=[("SV", last_dv)])
            else:
                au = act(FA[4][:], UBUF[c % 2][:], AF.Copy, scale=dcsb[:, c:c + 1],
                         waits=[("SV", last_dv), ("SX", uld2[c])])
                dve(lambda e, c=c: e.tensor_add(f_yacc[c % 2][:], f_yacc[c % 2][:], FA[4][:]),
                    waits=[("SA", au)])
                v2 = dve(lambda e, c=c: e.tensor_mul(
                    UBUF[c % 2][:], f_yacc[c % 2][:], XIP[c % 2][:, 0:T]),
                    waits=[("SX", zld[c])])
                last_dv = v2
                ygout[c] = dma("sync", yg_dram[c * 128:(c + 1) * 128, :], UBUF[c % 2][:],
                               waits=[("SV", v2)])
    w1_feed(G1)
    w1_last_hev = hev_at[G1 - 1]
    w1_h_parked = hr_dma[0] if hr_dma[0] > hr_dma[1] else hr_dma[1]

    if stop < 8:
        _fin(E, nc, es)
        return nc

    op_ev = 0
    ygmm = {}
    hg_first = [("SA", w1_last_hev), ("SP", E.c("SP")), ("SV", last_dv)]
    for tt in range(NTT):
        for kc in range(NCI):
            prev = (tt, kc - 2) if kc >= 2 else (tt - 1, NCI - 2 + kc)
            ygl = dma("sync", UBUF[kc % 2][:, 0:512],
                      yg_dram[kc * 128:(kc + 1) * 128, tt * 512:(tt + 1) * 512],
                      waits=[("SX", ygout[NCI - 1]),
                             ("SM", ygmm.get(prev, 0))])
            sl, aw = ring1.load(wout_h[kc], q="gpsimd")
            for mcq in range(8):
                pt = (PA if mcq < 4 else PB)[:, (mcq % 4) * 512:(mcq % 4 + 1) * 512]
                mm(pt, WB1[sl][:, mcq * 128:(mcq + 1) * 128], UBUF[kc % 2][:, 0:512],
                   start=(kc == 0), stop=(kc == NCI - 1),
                   waits=[("SA", aw), ("SX", ygl), ("SA", op_ev if op_ev else w1_last_hev),
                          ("SV", last_dv)],
                   inc=(kc == NCI - 1 and mcq == 7))
            ring1.consumed(sl)
            ygmm[(tt, kc)] = E.c("SM")
        gmm = E.c("SM")
        for mcq in range(8):
            pt = (PA if mcq < 4 else PB)[:, (mcq % 4) * 512:(mcq % 4 + 1) * 512]
            op_ev = act(BA[mcq][:, tt * 512:(tt + 1) * 512], pt, AF.Gelu,
                        waits=[("SM", gmm)] + (hg_first if tt == 0 else []))
    hg_done = op_ev

    if stop < 9:
        _fin(E, nc, es)
        return nc

    glu_a = 0
    glu_b_a = 0
    glu_v = 0
    s_parked = 0
    for m in range(8):
        sl2, aw2 = ring1.load(wglu_h[8 + m])
        for kc in range(NKC):
            for tt in range(NTT):
                mm(PA[:, tt * 512:(tt + 1) * 512], WB1[sl2][:, kc * 128:(kc + 1) * 128],
                   BA[kc][:, tt * 512:(tt + 1) * 512],
                   start=(kc == 0), stop=(kc == NKC - 1),
                   waits=[("SA", aw2), ("SA", hg_done), ("SA", glu_a)],
                   inc=(kc == NKC - 1 and tt == NTT - 1))
        g2mm = E.c("SM")
        ring1.consumed(sl2)
        sg2 = act(FA[0][:], PA[:], AF.Sigmoid, bias=glubsb[:, 8 + m:9 + m],
                  waits=[("SM", g2mm), ("SV", glu_v)])
        glu_a = sg2
        sl1, aw1 = ring1.load(wglu_h[m])
        for kc in range(NKC):
            for tt in range(NTT):
                mm(PB[:, tt * 512:(tt + 1) * 512], WB1[sl1][:, kc * 128:(kc + 1) * 128],
                   BA[kc][:, tt * 512:(tt + 1) * 512],
                   start=(kc == 0), stop=(kc == NKC - 1),
                   waits=[("SA", aw1), ("SV", glu_b_a)],
                   inc=(kc == NKC - 1 and tt == NTT - 1))
        g1mm = E.c("SM")
        ring1.consumed(sl1)
        a1 = dve(lambda e, m=m: e.tensor_scalar(
            FA[1][:], PB[:], glubsb[:, m:m + 1], None, ALU.add),
            waits=[("SM", g1mm), ("SV", glu_v)])
        glu_b_a = a1
        v1 = dve(lambda e: e.tensor_mul(FA[1][:], FA[1][:], FA[0][:]),
                 waits=[("SV", a1), ("SA", sg2)])
        v2 = dve(lambda e, m=m: e.tensor_add(FA[1][:], FA[1][:], XNBF[m][:]))
        glu_v = v2
        s_parked = dma("sync", s_dram[m * 128:(m + 1) * 128, :], FA[1][:],
                       waits=[("SV", v2)])

    if stop < 11:
        _fin(E, nc, es)
        return nc

    PS4W = [PA[:, 0:1024], PA[:, 1024:2048], PB[:, 0:1024], PB[:, 1024:2048]]
    pgV_v = [glu_v] * 4
    wbc_ld = 0
    moe_pool = 0
    y_ev = 0
    HD = [BA, XNBF]
    hld = {}
    pool_at = {}
    sl_mm = {}
    NSL = NE * 4

    def h_loads(si):
        if si >= NSL:
            return
        e_, fs = si // 4, si % 4
        dst = HD[si % 2]
        for kc in range(8):
            row = (e_ * 32 + fs * 8 + kc) * 128
            w_ = [("SM", sl_mm.get(si - 2, 0)), ("SP", pool_at.get((si - 2, kc), 0))]
            if si == 0:
                w_ += [("SM", E.c("SM")), ("SA", glu_a), ("SV", glu_b_a)]
            elif si == 1:
                w_ += [("SV", glu_v)]
            hld[(si, kc)] = dma("sync", dst[kc][:], h_dram[row:row + 128, :],
                                waits=w_ + [("SX", w1_h_parked)])

    h_loads(0)
    for si in range(NSL):
        e_, fs = si // 4, si % 4
        dst = HD[si % 2]
        if fs == 0:
            wbc_ld = bcast("gpsimd", WBC[:], w8_dram[e_:e_ + 1, :],
                           waits=[("SX", d_w8), ("SP", moe_pool),
                                  ("SV", last_dv if e_ == 0 else 0)])
        h_loads(si + 1)
        for kc in range(8):
            moe_pool = pool(lambda e, dst=dst, kc=kc: e.tensor_mul(dst[kc][:], dst[kc][:], WBC[:]),
                            waits=[("SX", hld[(si, kc)]), ("SG", wbc_ld),
                                   ("SM", sl_mm.get(si - 2, 0))])
            pool_at[(si, kc)] = moe_pool
        for md in range(8):
            sl2, aw2 = ring2.load(w2_h[e_, fs, md])
            fold = (e_ == 0 and fs == 3)
            for th in range(2):
                g4 = (md * 2 + th) % 4
                pt4 = PS4W[g4]
                for kc in range(8):
                    for t2 in range(2):
                        mm(pt4[:, t2 * 512:(t2 + 1) * 512],
                           WB2[sl2][:, kc * 128:(kc + 1) * 128],
                           dst[kc][:, th * 1024 + t2 * 512:th * 1024 + (t2 + 1) * 512],
                           start=(kc == 0), stop=(kc == 7 and not fold),
                           waits=[("SA", aw2), ("SP", pool_at[(si, kc)]),
                                  ("SV", pgV_v[g4]),
                                  ("SA", glu_a if si == 0 else 0)],
                           inc=(kc == 7 and t2 == 1 and not fold))
                if fold:
                    for t2 in range(2):
                        mm(pt4[:, t2 * 512:(t2 + 1) * 512],
                           b2b[:, md * 128:(md + 1) * 128],
                           w8bf_v[:, th * 1024 + t2 * 512:th * 1024 + (t2 + 1) * 512],
                           start=False, stop=True,
                           waits=[("SV", w8c), ("SV", b2cast)],
                           inc=(t2 == 1))
                gmm2 = E.c("SM")
                if si == 0:
                    y_ev = dve(lambda e, md=md, th=th, pt4=pt4: e.tensor_copy(
                        FA[md][:, th * 1024:(th + 1) * 1024], pt4),
                        waits=[("SM", gmm2)])
                else:
                    y_ev = dve(lambda e, md=md, th=th, pt4=pt4: e.tensor_add(
                        FA[md][:, th * 1024:(th + 1) * 1024],
                        FA[md][:, th * 1024:(th + 1) * 1024], pt4),
                        waits=[("SM", gmm2)])
                pgV_v[g4] = y_ev
            ring2.consumed(sl2)
        sl_mm[si] = E.c("SM")
    macc_v = y_ev

    if stop < 12:
        _fin(E, nc, es)
        return nc

    out_done = 0
    ag_a = 0
    for md in range(8):
        ag = act(WBC[:], FA[md][:], AF.Silu,
                 waits=[("SV", macc_v if md == 0 else 0), ("SV", E.c("SV")),
                        ("SP", moe_pool), ("SG", wbc_ld)])
        s1 = dma("sync", WST[0][:], s_dram[md * 128:(md + 1) * 128, 0:1024],
                 waits=[("SX", s_parked), stage_free[0],
                        ("SV", E.c("SV") if md >= 1 else 0)])
        s2 = dma("sync", WST[1][:], s_dram[md * 128:(md + 1) * 128, 1024:2048],
                 waits=[stage_free[1]])
        stage_free[0] = stage_free[1] = ("SA", ag)
        va = dve(lambda e: e.tensor_mul(WBC[:, 0:1024], WBC[:, 0:1024], WST[0][:]),
                 waits=[("SA", ag), ("SX", s2)])
        vb = dve(lambda e: e.tensor_mul(WBC[:, 1024:2048], WBC[:, 1024:2048], WST[1][:]))
        xr = dma("sync", FA[md][:], x_h[md * 128:(md + 1) * 128, :],
                 waits=[("SA", ag)])
        vc = dve(lambda e, md=md: e.tensor_add(FA[md][:], FA[md][:], WBC[:]),
                 waits=[("SX", xr), ("SV", vb)])
        out_done = dma("sync", out_h[md * 128:(md + 1) * 128, :], FA[md][:],
                       waits=[("SV", vc)])
        ag_a = ag
    E.add("sync", None, waits=[("SX", out_done)])

    _fin(E, nc, es)
    return nc


def _fin(E, nc, es):
    with ExitStack() as sem_es:
        sems = {s: sem_es.enter_context(nc.semaphore(s)) for s in SEMS}
        with nc.Block() as block:
            @block.sync
            def _(e):
                E.emit_stream("sync", e, sems)

            @block.tensor
            def _(e):
                E.emit_stream("tensor", e, sems)

            @block.vector
            def _(e):
                E.emit_stream("vector", e, sems)

            @block.scalar
            def _(e):
                E.emit_stream("scalar", e, sems)

            @block.gpsimd
            def _(e):
                E.emit_stream("gpsimd", e, sems)
    es.close()



def _prep_weights(inputs):
    f32 = np.float32
    W = {}
    win = np.asarray(inputs["L0_attn_in_proj_w"], f32)
    W["win"] = np.ascontiguousarray(win.reshape(8, 128, 32, 128).transpose(2, 0, 1, 3))
    W["wxp"] = np.ascontiguousarray(np.asarray(inputs["L0_attn_x_proj_w"], f32).reshape(16, 128, 96))
    W["wdt"] = np.ascontiguousarray(np.asarray(inputs["L0_attn_dt_proj_w"], f32).reshape(64, 16, 128).transpose(1, 0, 2))
    wout = np.asarray(inputs["L0_attn_out_proj_w"], f32)
    W["wout"] = np.ascontiguousarray(wout.reshape(16, 128, 8, 128).transpose(0, 2, 1, 3))
    wglu = np.asarray(inputs["L0_attn_glu_w"], f32)
    W["wglu"] = np.ascontiguousarray(wglu.reshape(8, 128, 16, 128).transpose(2, 0, 1, 3))
    w1 = np.asarray(inputs["L0_ffn_W1"], f32)
    W["w1"] = np.ascontiguousarray(w1.reshape(8, 8, 128, 4, 8, 128).transpose(0, 3, 4, 1, 2, 5))
    w2 = np.asarray(inputs["L0_ffn_W2"], f32)
    W["w2"] = np.ascontiguousarray(w2.reshape(8, 4, 8, 128, 8, 128).transpose(0, 1, 4, 2, 3, 5))
    W["wr"] = np.ascontiguousarray(np.asarray(inputs["L0_ffn_Wr"], f32).reshape(8, 128, 8))
    W["gcol"] = np.ascontiguousarray(np.asarray(inputs["L0_attn_ln_gamma"], f32).reshape(8, 128).T)
    W["bcol"] = np.ascontiguousarray(np.asarray(inputs["L0_attn_ln_beta"], f32).reshape(8, 128).T)
    W["cw"] = np.ascontiguousarray(np.asarray(inputs["L0_attn_conv_w"], f32).reshape(16, 128, 4))
    W["cb"] = np.ascontiguousarray(np.asarray(inputs["L0_attn_conv_b"], f32).reshape(16, 128).T)
    W["dtb"] = np.ascontiguousarray(np.asarray(inputs["L0_attn_dt_proj_b"], f32).reshape(16, 128).T)
    W["alog"] = np.ascontiguousarray(np.asarray(inputs["L0_attn_A_log"], f32).reshape(16, 128, 16))
    W["dcol"] = np.ascontiguousarray(np.asarray(inputs["L0_attn_D"], f32).reshape(16, 128).T)
    b1 = np.asarray(inputs["L0_ffn_b1"], f32)
    W["b1c"] = np.ascontiguousarray(b1.reshape(8, 32, 128).transpose(0, 2, 1))
    W["b2"] = np.ascontiguousarray(np.asarray(inputs["L0_ffn_b2"], f32))
    W["glub"] = np.ascontiguousarray(np.asarray(inputs["L0_attn_glu_b"], f32).reshape(16, 128).T)
    return W


def kernel(**inputs):
    W = _prep_weights(inputs)
    x = np.asarray(inputs["L0_attn_x"], np.float32)
    nc = build()
    in_maps = []
    for b in range(8):
        m = dict(W)
        m["x"] = np.ascontiguousarray(x[b].T)
        m["coremask"] = (np.arange(8) < b).astype(np.float32).reshape(8, 1)
        in_maps.append(m)
    res = run_bass_kernel_spmd(nc, in_maps, core_ids=list(range(8)))
    out = np.stack([res.results[b]["out"].T for b in range(8)], axis=0)
    return np.ascontiguousarray(out, dtype=np.float32)
